# version 24
# baseline (speedup 1.0000x reference)
import contextlib
import numpy as np

import concourse.bass as bass
import concourse.mybir as mybir
from concourse.bass_utils import run_bass_kernel_spmd

F32 = mybir.dt.float32
BF16 = mybir.dt.bfloat16
AF = mybir.ActivationFunctionType
ALU = mybir.AluOpType

P = 128
L = 4096
DM = 1024
DI = 2048
NST = 16
TO = 512
NCH = L // TO
NJ = DI // P
NI = DM // P


class Prog:

    ENGS = ("tensor", "vector", "scalar", "sync", "gpsimd")
    CHANNELS = ("wt0", "wt1", "w2a", "w2b", "wo0", "wo1", "wf0", "wf1",
                "xs", "ot0", "ot1", "ot2", "ot3")

    def __init__(self, nc, ctx):
        self.nc = nc
        self.ops = {e: [] for e in self.ENGS}
        self.sems = {e: ctx.enter_context(nc.semaphore("sem_" + e)) for e in self.ENGS}
        for ch in self.CHANNELS:
            self.sems[ch] = ctx.enter_context(nc.semaphore("sem_" + ch))
        self.counts = {s: 0 for s in self.sems}

    def add(self, eng, fn, waits=(), inc=False, inck=None, sem=None):
        tok = None
        s = sem if sem is not None else eng
        k = inck if inck is not None else (16 if eng == "sync" else 1)
        if inc:
            self.counts[s] += k
            tok = (s, self.counts[s])
        ws = []
        for w in waits:
            if w is None:
                continue
            if isinstance(w, list):
                ws.extend(x for x in w if x is not None)
            else:
                ws.append(w)
        self.ops[eng].append((ws, fn, (s, k) if inc else None))
        return tok

    def emit(self, block):
        for eng in self.ENGS:
            ops = self.ops[eng]
            if not ops:
                continue
            sems = self.sems

            def make(ops, eng):
                def body(e):
                    seen = {}
                    for ws, fn, incsk in ops:
                        red = {}
                        for (psem, v) in ws:
                            if v > red.get(psem, -1):
                                red[psem] = v
                        for psem, v in red.items():
                            if seen.get(psem, 0) >= v:
                                continue
                            seen[psem] = v
                            e.wait_ge(sems[psem], v)
                        last = fn(e)
                        if incsk is not None:
                            s, k = incsk
                            if last is None:
                                last = e.engine_nop() if eng in ("vector", "gpsimd") else e.nop()
                            last.then_inc(sems[s], k)
                return body
            getattr(block, eng)(make(ops, eng))


def build_nc():
    nc = bass.Bass("TRN2", target_bir_lowering=False)

    xin = nc.dram_tensor("xin", [L, DM], F32, kind="ExternalInput").ap()
    w_in = nc.dram_tensor("w_in", [DM, 2 * DI], F32, kind="ExternalInput").ap()
    conv_k = nc.dram_tensor("conv_k", [4, DI], F32, kind="ExternalInput").ap()
    cb_perm = nc.dram_tensor("cb_perm", [P, NJ], F32, kind="ExternalInput").ap()
    wxp_perm = nc.dram_tensor("wxp_perm", [P, NJ * 160], F32, kind="ExternalInput").ap()
    w_dt = nc.dram_tensor("w_dt", [P, DI], F32, kind="ExternalInput").ap()
    bdt_perm = nc.dram_tensor("bdt_perm", [P, NJ], F32, kind="ExternalInput").ap()
    a_perm = nc.dram_tensor("a_perm", [P, NJ * NST], F32, kind="ExternalInput").ap()
    d_perm = nc.dram_tensor("d_perm", [P, NJ], F32, kind="ExternalInput").ap()
    w_out = nc.dram_tensor("w_out", [DI, DM], F32, kind="ExternalInput").ap()
    w_ffn = nc.dram_tensor("w_ffn", [DM, DM], F32, kind="ExternalInput").ap()
    identc = nc.dram_tensor("identc", [P, P], F32, kind="ExternalInput").ap()
    onescol = nc.dram_tensor("onescol", [P, 2], F32, kind="ExternalInput").ap()
    onesrow = nc.dram_tensor("onesrow", [1, P], F32, kind="ExternalInput").ap()
    selc = nc.dram_tensor("selc", [32, 2 * DI], F32, kind="ExternalInput").ap()
    yout = nc.dram_tensor("yout", [L, DM], F32, kind="ExternalOutput").ap()

    wt_ld = nc.dram_tensor("wt_ld", [NJ, P, 32 * P], BF16)
    w2_ld = nc.dram_tensor("w2_ld", [NJ, P, NI * P], BF16)
    wo_ld = nc.dram_tensor("wo_ld", [NI, P, NJ * P], BF16)
    wf_ld = nc.dram_tensor("wf_ld", [NI, P, NI * P], BF16)

    with contextlib.ExitStack() as ctx:
        ent = ctx.enter_context
        pr = Prog(nc, ctx)
        A = pr.add
        p0ch = {}

        def SD(fn, waits=(), ch="sync"):
            t = pr.add("sync", fn, waits=list(waits) + [p0ch.get(ch)],
                       inc=True, sem=ch if ch != "sync" else None)
            p0ch[ch] = t
            return t

        psr = [ent(nc.psum_tensor(f"psr{i}", [P, TO], F32)) for i in range(4)]
        psx0 = ent(nc.psum_tensor("psx0", [P, TO], F32))
        psx1 = ent(nc.psum_tensor("psx1", [32, TO], F32))
        pss0 = ent(nc.psum_tensor("pss0", [1, TO], F32))
        pss1 = ent(nc.psum_tensor("pss1", [1, TO], F32))
        rot_free = [None] * 4
        rot_i = [0]

        def rot_acquire():
            r = rot_i[0] % 4
            rot_i[0] += 1
            return r, rot_free[r]

        ident = ent(nc.sbuf_tensor("ident", [P, P], F32))
        ones32 = ent(nc.sbuf_tensor("ones32", [P, 2], F32))
        ones16 = ent(nc.sbuf_tensor("ones16", [P, 2], BF16))
        onesr32 = ent(nc.sbuf_tensor("onesr32", [1, P], F32))
        onesr16 = ent(nc.sbuf_tensor("onesr16", [1, P], BF16))
        sel16 = ent(nc.sbuf_tensor("sel16", [32, 2 * DI], BF16))
        apos = ent(nc.sbuf_tensor("apos", [P, NJ * NST], F32))
        cbias = ent(nc.sbuf_tensor("cbias", [P, NJ], F32))
        nbdt = ent(nc.sbuf_tensor("nbdt", [P, NJ], F32))
        dvec = ent(nc.sbuf_tensor("dvec", [P, NJ], F32))
        wxp16 = ent(nc.sbuf_tensor("wxp16", [P, NJ * 160], BF16))
        wdt16 = ent(nc.sbuf_tensor("wdt16", [P, DI], BF16))

        t_id = SD(lambda e: e.dma_start(out=ident[:, :], in_=identc), ch="ot0")
        t_oc = SD(lambda e: e.dma_start(out=ones32[:, :], in_=onescol), ch="ot1")
        t_or = SD(lambda e: e.dma_start(out=onesr32[:, :], in_=onesrow), ch="ot2")
        t_o16 = A("vector", lambda e: e.tensor_copy(out=ones16[:, :], in_=ones32[:, :]), waits=[t_oc], inc=True)
        t_or16 = A("vector", lambda e: e.tensor_copy(out=onesr16[:, :], in_=onesr32[:, :]),
                   waits=[t_or], inc=True)
        t = SD(lambda e: e.dma_start(out=apos[:, :], in_=a_perm), ch="ot3")
        t_ap = A("scalar", lambda e: e.activation(out=apos[:, :], in_=apos[:, :], func=AF.Exp), waits=[t], inc=True)
        t_cb = SD(lambda e: e.dma_start(out=cbias[:, :], in_=cb_perm), ch="xs")
        t = SD(lambda e: e.dma_start(out=nbdt[:, :], in_=bdt_perm), ch="ot0")
        t_nb = A("vector", lambda e: e.tensor_scalar_mul(out=nbdt[:, :], in0=nbdt[:, :],
                                                         scalar1=-1.0), waits=[t], inc=True)
        t_dv = SD(lambda e: e.dma_start(out=dvec[:, :], in_=d_perm), ch="ot1")

        with nc.sbuf_tensor("p0sel", [32, 2 * DI], F32) as p0sel, \
             nc.sbuf_tensor("p0big", [P, 4096], F32) as p0big, \
             nc.sbuf_tensor("p0kb", [P, 4 * DI], F32) as p0kb, \
             nc.sbuf_tensor("p0ck", [1, DI], F32) as p0ck, \
             nc.sbuf_tensor("p0wst", [P, 16 * P], F32) as p0wst, \
             nc.sbuf_tensor("p0wj", [P, 4 * 32 * P], BF16) as p0wj, \
             nc.sbuf_tensor("p0w2", [P, 4 * NI * P], BF16) as p0w2, \
             nc.sbuf_tensor("p0wo", [P, DM], BF16) as p0wo:

            t = SD(lambda e: e.dma_start(out=p0sel[:, :], in_=selc), ch="ot2")
            t_sel = A("vector", lambda e: e.tensor_copy(out=sel16[:, :], in_=p0sel[:, :]), waits=[t], inc=True)
            t = SD(lambda e: e.dma_start(out=p0big[:, :NJ * 160], in_=wxp_perm), ch="ot3")
            t_wxp = A("vector", lambda e: e.tensor_copy(out=wxp16[:, :], in_=p0big[:, :NJ * 160]), waits=[t], inc=True)
            t = SD(lambda e: e.dma_start(out=p0big[:, :DI], in_=w_dt),
                  waits=[t_wxp], ch="ot3")
            t_wdt = A("vector", lambda e: e.tensor_copy(out=wdt16[:, :], in_=p0big[:, :DI]), waits=[t], inc=True)

            tprev_ck = t_wdt
            tv = None
            for k in range(4):
                tck = SD(lambda e, k=k: e.dma_start(out=p0ck[:, :], in_=conv_k[k:k + 1, :]),
                        waits=[tprev_ck])
                for q in range(4):
                    r, fr = rot_acquire()
                    tm = A("tensor", lambda e, k=k, q=q, r=r: e.matmul(
                        psr[r][:, :], onesr32[0:1, :], p0ck[0:1, q * TO:(q + 1) * TO],
                        start=True, stop=True), waits=[tck, fr, t_or], inc=True)
                    tv = A("vector", lambda e, k=k, q=q, r=r: e.tensor_copy(
                        out=p0kb[:, k * DI + q * TO:k * DI + (q + 1) * TO], in_=psr[r][:, :]),
                        waits=[tm], inc=True)
                    rot_free[r] = tv
                tprev_ck = tv
            t_kb = tv

            wt_wr = [None] * NJ
            w2_wr = [None] * NJ
            t_wj_dma = None
            t_w2_dma = None
            last_tt = None
            last_c2 = None
            Q = 4
            for j4 in range(0, NJ, Q):
                ltt = [None] * Q
                lc2 = [None] * Q
                for i in range(NI):
                    sl = i % 2
                    td = SD(lambda e, j4=j4, i=i, sl=sl: e.dma_start(
                        out=p0wst[:, sl * Q * P:(sl + 1) * Q * P],
                        in_=w_in[i * P:(i + 1) * P, j4 * P:(j4 + Q) * P]),
                        waits=[last_tt if i >= 2 else None, t_wj_dma],
                        ch="wt1" if sl == 0 else "w2b")
                    for jq in range(Q):
                        j = j4 + jq
                        for k in range(4):
                            last_tt = A("vector", lambda e, jq=jq, j=j, i=i, k=k, sl=sl: e.tensor_tensor(
                                out=p0wj[:, (jq * 32 + i * 4 + k) * P:(jq * 32 + i * 4 + k + 1) * P],
                                in0=p0wst[:, (sl * Q + jq) * P:(sl * Q + jq + 1) * P],
                                in1=p0kb[:, k * DI + j * P:k * DI + (j + 1) * P], op=ALU.mult),
                                waits=[td, t_kb], inc=True)
                        ltt[jq] = last_tt
                    sl2 = i % 2 + 2
                    td2 = SD(lambda e, j4=j4, i=i, sl2=sl2: e.dma_start(
                        out=p0wst[:, sl2 * Q * P:(sl2 + 1) * Q * P],
                        in_=w_in[i * P:(i + 1) * P, DI + j4 * P:DI + (j4 + Q) * P]),
                        waits=[last_c2 if i >= 2 else None, t_w2_dma],
                        ch="wo1" if sl2 == 2 else "wf1")
                    for jq in range(Q):
                        last_c2 = A("vector", lambda e, jq=jq, i=i, sl2=sl2: e.tensor_copy(
                            out=p0w2[:, (jq * NI + i) * P:(jq * NI + i + 1) * P],
                            in_=p0wst[:, (sl2 * Q + jq) * P:(sl2 * Q + jq + 1) * P]),
                            waits=[td2], inc=True)
                        lc2[jq] = last_c2
                for jq in range(Q):
                    j = j4 + jq
                    t_wj_dma = SD(lambda e, j=j, jq=jq: e.dma_start(
                        out=wt_ld.ap()[j], in_=p0wj[:, jq * 32 * P:(jq + 1) * 32 * P]),
                        waits=[ltt[jq]], ch="wt0")
                    t_w2_dma = SD(lambda e, j=j, jq=jq: e.dma_start(
                        out=w2_ld.ap()[j], in_=p0w2[:, jq * NI * P:(jq + 1) * NI * P]),
                        waits=[lc2[jq]], ch="w2a")
                    wt_wr[j] = t_wj_dma
                    w2_wr[j] = t_w2_dma

            t_wo_dma = None
            tcast = None
            for j in range(NJ):
                td = SD(lambda e, j=j: e.dma_start(
                    out=p0big[:, :DM], in_=w_out[j * P:(j + 1) * P, :]),
                    waits=[tcast, t_wxp, t_wo_dma,
                           t_wdt if j == 0 else None],
                    ch="ot0" if j % 2 == 0 else "ot1")
                tcast = A("vector", lambda e: e.tensor_copy(out=p0wo[:, :], in_=p0big[:, :DM]),
                          waits=[td, t_wo_dma], inc=True)
                for o in range(NI):
                    t_wo_dma = SD(lambda e, j=j, o=o: e.dma_start(
                        out=wo_ld.ap()[o][:, j * P:(j + 1) * P],
                        in_=p0wo[:, o * P:(o + 1) * P]), waits=[tcast], ch="wo0")
            wo_wr = t_wo_dma

            t_sgn = None
            t_wf_dma = None
            for ot in range(NI):
                td = SD(lambda e, ot=ot: e.dma_start(
                    out=p0big[:, :DM], in_=w_ffn[ot * P:(ot + 1) * P, :]),
                    waits=[t_sgn, tcast if ot == 0 else None,
                           t_wo_dma if ot == 0 else None],
                    ch="ot2" if ot % 2 == 0 else "ot3")
                for ib in range(NI):
                    r, fr = rot_acquire()
                    tm = A("tensor", lambda e, ib=ib, r=r: e.transpose(
                        psr[r][:, :P], p0big[:, ib * P:(ib + 1) * P], ident[:, :]),
                        waits=[td, fr, t_id], inc=True)
                    t_sgn = A("scalar", lambda e, ib=ib, r=r: e.activation(
                        out=p0wo[:, ib * P:(ib + 1) * P],
                        in_=psr[r][:, :P], func=AF.Sign),
                        waits=[tm, t_wf_dma], inc=True)
                    rot_free[r] = t_sgn
                t_wf_dma = SD(lambda e, ot=ot: e.dma_start(
                    out=wf_ld.ap()[ot], in_=p0wo[:, :]), waits=[t_sgn], ch="wf0")
            wf_wr = t_wf_dma

        wtbuf = ent(nc.sbuf_tensor("wtbuf", [P, 2 * 32 * P], BF16))
        w2buf = ent(nc.sbuf_tensor("w2buf", [P, 2 * NI * P], BF16))
        wobuf = ent(nc.sbuf_tensor("wobuf", [P, 2 * NJ * P], BF16))
        wfbuf = ent(nc.sbuf_tensor("wfbuf", [P, 2 * NI * P], BF16))
        xstag = ent(nc.sbuf_tensor("xstag", [P, DM], F32))
        x16 = ent(nc.sbuf_tensor("x16", [P, 2 * NI, TO + 3], BF16))
        u16 = ent(nc.sbuf_tensor("u16", [P, NJ, TO], BF16))
        dtl16 = ent(nc.sbuf_tensor("dtl16", [P, TO], BF16))
        bc16 = ent(nc.sbuf_tensor("bc16", [32, TO], BF16))
        bneg = ent(nc.sbuf_tensor("bneg", [P, NST, TO], BF16))
        cpos = ent(nc.sbuf_tensor("cpos", [P, NST, TO], BF16))
        dtneg = ent(nc.sbuf_tensor("dtneg", [P, 2 * TO], F32))
        szb = ent(nc.sbuf_tensor("szb", [P, 2 * TO], BF16))
        w16b = ent(nc.sbuf_tensor("w16b", [P, 2 * TO], BF16))
        dab = ent(nc.sbuf_tensor("dab", [P, 8 * TO], BF16))
        dbb = ent(nc.sbuf_tensor("dbb", [P, 2 * TO], BF16))
        hb = ent(nc.sbuf_tensor("hb", [P, 4 * TO], BF16))
        yjb = ent(nc.sbuf_tensor("yjb", [P, 2 * TO], BF16))
        ymix = ent(nc.sbuf_tensor("ymix", [P, NJ, TO], BF16))
        sqtmp = ent(nc.sbuf_tensor("sqtmp", [P, 2 * TO], BF16))
        x1b = ent(nc.sbuf_tensor("x1b", [P, NI, TO], BF16))
        outf = ent(nc.sbuf_tensor("outf", [P, 2 * TO], F32))
        otok = ent(nc.sbuf_tensor("otok", [P, 4 * DM], F32))
        state = ent(nc.sbuf_tensor("state", [P, NJ * NST], F32))
        stt = ent(nc.sbuf_tensor("stt", [1, 3 * TO], F32))
        epsc = ent(nc.sbuf_tensor("epsc", [1, 2], F32))
        tinyb = ent(nc.sbuf_tensor("tinyb", [P, 2], F32))
        stt16 = ent(nc.sbuf_tensor("stt16", [1, 2 * TO], BF16))
        abuf = ent(nc.sbuf_tensor("abuf", [P, 2 * TO], BF16))
        abuf2 = ent(nc.sbuf_tensor("abuf2", [P, 2 * TO], F32))
        flsh = ent(nc.sbuf_tensor("flsh", [P, 8], F32))

        for _eng in ("vector", "scalar", "tensor", "gpsimd"):
            A(_eng, lambda e: None, waits=list(p0ch.values()), inc=True)
        t_ms = A("vector", lambda e: e.memset(state[:, :], 0.0), inc=True)
        t_eps = A("vector", lambda e: e.memset(epsc[0:1, :], 1e-5), inc=True)
        t_tb = A("vector", lambda e: e.memset(tinyb[:, :], 1e-30), inc=True)
        t_mx = A("vector", lambda e: e.memset(x16[:, :, 0:3], 0.0), inc=True)

        tk_x16_pp = {0: [t_mx], 1: [t_mx]}
        tk_xz_hist = {0: None, 1: None}
        tk_tail = {0: t_mx, 1: t_mx}
        tk_u16_j = [None] * NJ
        tk_uxp = [None]
        tk_dtl_free = [None]
        tk_bc_free = [None]
        tk_bcpos_free = [None]
        tk_ymix_free = [None]
        tk_x1_free = [None]
        tk_psx_free = [None]
        tk_pss_free = [None, None]
        tk_wt_slot = [wt_wr[NJ - 1], [wt_wr[NJ - 1], p0ch.get("wt1")]]
        tk_w2_slot = [w2_wr[NJ - 1], [w2_wr[NJ - 1], p0ch.get("w2b")]]
        tk_wo_slot = [wo_wr, [wo_wr, p0ch.get("wo1")]]
        tk_wf_slot = [wf_wr, [wf_wr, p0ch.get("wf1")]]
        tk_xs_slot = [p0ch.get("xs")]
        tk_ot_slot = [p0ch.get("ot0"), p0ch.get("ot1"), p0ch.get("ot2"), p0ch.get("ot3")]
        tk_sz_slot = [None] * 2
        tk_dtn_slot = [None] * 2
        tk_da_slot = [None] * 8
        tk_sq_slot = [None, None]
        tk_outf_slot = [None] * 2
        tk_ln_free = [None]
        tk_dbb = [None, None]
        tk_hb = [None] * 4
        tk_yjb = [None, None]
        tk_w16 = [None, None]
        tk_dtn_act = [None, None]
        tk_state_last = [None]

        st = {}
        BARRIER = [False]

        def hard_sync():
            if not BARRIER[0]:
                return
            toks = [(e, pr.counts[e]) for e in ("tensor", "vector", "scalar")
                    if pr.counts[e] > 0]
            for e in ("tensor", "vector", "scalar"):
                A(e, lambda e2: None, waits=toks, inc=True)

        def front_a(c):
            t0 = c * TO
            pc = c % 2
            xo = pc * NI
            s = st[c] = {}
            ta_last = None
            for tt in range(4):
                td = A("sync", lambda e, t0=t0, tt=tt: e.dma_start(
                    out=xstag[:, :], in_=xin[t0 + tt * P:t0 + (tt + 1) * P, :]),
                    waits=[tk_xs_slot[0]], inc=True, sem="xs")
                tm = None
                for ib in range(NI):
                    r, fr = rot_acquire()
                    tm = A("tensor", lambda e, ib=ib, r=r: e.transpose(
                        psr[r][:, :P], xstag[:, ib * P:(ib + 1) * P], ident[:, :]),
                        waits=[td, fr] + tk_x16_pp[pc], inc=True)
                    ta_last = A("scalar", lambda e, tt=tt, ib=ib, r=r, xo=xo: e.activation(
                        out=x16[:, xo + ib, 3 + tt * P:3 + (tt + 1) * P], in_=psr[r][:, :P],
                        func=AF.Copy), waits=[tm] + tk_x16_pp[pc], inc=True)
                    rot_free[r] = ta_last
                tk_xs_slot[0] = tm
            s["ta_last"] = ta_last
            s["t_uev"] = [None] * NJ
            s["t_xp"] = None
            s["t_xz_last"] = None

        def front_b(c, j):
            pc = c % 2
            xo = pc * NI
            s = st[c]
            ta_last = s["ta_last"]
            sl = j % 2
            tdw = A("sync", lambda e, j=j, sl=sl: e.dma_start(
                out=wtbuf[:, sl * 32 * P:(sl + 1) * 32 * P], in_=wt_ld.ap()[j]),
                waits=[tk_wt_slot[sl]], inc=True, sem="wt0" if sl == 0 else "wt1")
            r, fr = rot_acquire()

            def xz_mms(e, j=j, sl=sl, r=r, xo=xo):
                last = None
                for i in range(NI):
                    for k in range(4):
                        last = e.matmul(
                            psr[r][:, :],
                            wtbuf[:, (sl * 32 + i * 4 + k) * P:(sl * 32 + i * 4 + k + 1) * P],
                            x16[:, xo + i, k:k + TO],
                            start=(i == 0 and k == 0), stop=(i == NI - 1 and k == 3))
                return last
            t_xz_last = A("tensor", xz_mms, waits=[tdw, fr, ta_last, tk_tail[pc]],
                          inc=True)
            tk_wt_slot[sl] = t_xz_last
            s["t_xz_last"] = t_xz_last
            prev_xp = st[c - 1]["t_xp"] if c > 0 else None
            s["t_uev"][j] = A("scalar", lambda e, j=j, r=r: e.activation(
                out=u16[:, j, :], in_=psr[r][:, :], func=AF.Silu,
                bias=cbias[:, j:j + 1], scale=1.0),
                waits=[t_xz_last, t_cb, tk_u16_j[j], prev_xp], inc=True)
            rot_free[r] = s["t_uev"][j]


        def front_tail(c):
            pc = c % 2
            xo = pc * NI
            s = st[c]
            ta_last = s["ta_last"]

            def xp_mms(e):
                last = None
                for j in range(NJ):
                    e.matmul(psx0[:, :], wxp16[:, j * 160:j * 160 + P], u16[:, j, :],
                             start=(j == 0), stop=(j == NJ - 1))
                    last = e.matmul(psx1[:, :], wxp16[:, j * 160 + P:j * 160 + 160],
                                    u16[:, j, :], start=(j == 0), stop=(j == NJ - 1))
                return last
            t_xp = A("tensor", xp_mms,
                     waits=[s["t_uev"][NJ - 1], t_wxp]
                     + [t for t in tk_psx_free if t], inc=True)
            s["t_xp"] = t_xp
            tk_tail[1 - pc] = A("vector", lambda e, xo=xo, pc=pc: e.tensor_copy(
                out=x16[:, (1 - pc) * NI:(1 - pc) * NI + NI, 0:3],
                in_=x16[:, xo:xo + NI, TO:TO + 3]),
                waits=[ta_last, tk_xz_hist[1 - pc]], inc=True)
            tk_xz_hist[pc] = s["t_xz_last"]
            t_dtl = A("vector", lambda e: e.tensor_copy(out=dtl16[:, :], in_=psx0[:, :]),
                      waits=[t_xp] + tk_dtl_free, inc=True)
            t_bc = A("vector", lambda e: e.tensor_copy(out=bc16[:, :], in_=psx1[:, :]),
                     waits=[t_xp] + tk_bc_free, inc=True)
            tk_psx_free[:] = [t_dtl, t_bc]
            s["t_dtl"] = t_dtl
            t_bn = [None] * NST
            t_cp = [None] * NST
            t_bcast_pe = None
            for n in range(NST):
                r, fr = rot_acquire()
                t_bcast_pe = A("tensor", lambda e, n=n, r=r: e.matmul(
                    psr[r][:, :], sel16[:, n * P:(n + 1) * P], bc16[:, :],
                    start=True, stop=True), waits=[t_bc, t_sel, fr], inc=True)
                t_bn[n] = A("vector", lambda e, n=n, r=r: e.tensor_copy(
                    out=bneg[:, n, :], in_=psr[r][:, :]),
                    waits=[t_bcast_pe] + tk_bcpos_free, inc=True)
                rot_free[r] = t_bn[n]
                r, fr = rot_acquire()
                t_bcast_pe = A("tensor", lambda e, n=n, r=r: e.matmul(
                    psr[r][:, :], sel16[:, DI + n * P:DI + (n + 1) * P], bc16[:, :],
                    start=True, stop=True), waits=[t_bc, t_sel, fr], inc=True)
                t_cp[n] = A("vector", lambda e, n=n, r=r: e.tensor_copy(
                    out=cpos[:, n, :], in_=psr[r][:, :]),
                    waits=[t_bcast_pe] + tk_bcpos_free, inc=True)
                rot_free[r] = t_cp[n]
            tk_bc_free[:] = [t_bcast_pe]
            s["t_bn"], s["t_cp"] = t_bn, t_cp

        def scan(c, hooks=None):
            pc = c % 2
            xo = pc * NI
            s = st[c]
            t_uev, t_dtl = s["t_uev"], s["t_dtl"]
            t_bn, t_cp = s["t_bn"], s["t_cp"]
            ta_last = s["ta_last"]
            prev_state = tk_state_last[0]
            t_ymix = [None] * NJ
            t_dtp = None
            for j0 in range(0, NJ, 2):
                pair = (j0, j0 + 1)
                t_dtp_p, t_z_p, rr1, rr2 = {}, {}, {}, {}
                for jj in pair:
                    sl = jj % 2
                    r1, fr1 = rot_acquire()
                    t_dtp_p[jj] = A("tensor", lambda e, jj=jj, r1=r1: e.matmul(
                        psr[r1][:, :], wdt16[:, jj * P:(jj + 1) * P], dtl16[:, :],
                        start=True, stop=True), waits=[t_dtl, t_wdt, fr1], inc=True)
                    rr1[jj] = r1
                    tdw2 = A("sync", lambda e, jj=jj, sl=sl: e.dma_start(
                        out=w2buf[:, sl * NI * P:(sl + 1) * NI * P], in_=w2_ld.ap()[jj]),
                        waits=[tk_w2_slot[sl]], inc=True, sem="w2a" if sl == 0 else "w2b")
                    r2, fr2 = rot_acquire()

                    def z_mms(e, jj=jj, sl=sl, r2=r2, xo=xo):
                        last = None
                        for i in range(NI):
                            last = e.matmul(psr[r2][:, :],
                                            w2buf[:, (sl * NI + i) * P:(sl * NI + i + 1) * P],
                                            x16[:, xo + i, 3:3 + TO],
                                            start=(i == 0), stop=(i == NI - 1))
                        return last
                    t_z_p[jj] = A("tensor", z_mms, waits=[tdw2, fr2, ta_last], inc=True)
                    tk_w2_slot[sl] = t_z_p[jj]
                    rr2[jj] = r2
                t_sg_p = {}
                for jj in pair:
                    sl = jj % 2
                    t_sg_p[jj] = A("scalar", lambda e, jj=jj, sl=sl, r1=rr1[jj]: e.activation(
                        out=dtneg[:, sl * TO:(sl + 1) * TO], in_=psr[r1][:, :],
                        func=AF.Sigmoid, scale=-1.0, bias=nbdt[:, jj:jj + 1]),
                        waits=[t_dtp_p[jj], t_nb, tk_dtn_slot[sl], tk_dtn_act[sl]],
                        inc=True)
                    rot_free[rr1[jj]] = t_sg_p[jj]
                t_sz_p = {}
                for jj in pair:
                    sl = jj % 2
                    t_sz_p[jj] = A("scalar", lambda e, sl=sl, r2=rr2[jj]: e.activation(
                        out=szb[:, sl * TO:(sl + 1) * TO], in_=psr[r2][:, :], func=AF.Silu),
                        waits=[t_z_p[jj], tk_sz_slot[sl]], inc=True)
                    rot_free[rr2[jj]] = t_sz_p[jj]
                t_dtn_p = {}
                for jj in pair:
                    sl = jj % 2
                    t_dtn_p[jj] = A("scalar", lambda e, sl=sl: e.activation(
                        out=dtneg[:, sl * TO:(sl + 1) * TO],
                        in_=dtneg[:, sl * TO:(sl + 1) * TO], func=AF.Ln,
                        bias=tinyb[:, 0:1]), waits=[t_tb, t_sg_p[jj]], inc=True)
                for jj in pair:
                    sl = jj % 2
                    t_w16 = A("vector", lambda e, jj=jj, sl=sl: e.tensor_tensor(
                        out=w16b[:, sl * TO:(sl + 1) * TO],
                        in0=dtneg[:, sl * TO:(sl + 1) * TO],
                        in1=u16[:, jj, :], op=ALU.mult),
                        waits=[t_dtn_p[jj], t_uev[jj], tk_w16[sl]], inc=True)
                    tk_dtn_slot[sl] = t_w16
                    t_acc = A("vector", lambda e, jj=jj, sl=sl: e.tensor_scalar(
                        out=yjb[:, sl * TO:(sl + 1) * TO], in0=u16[:, jj, :],
                        scalar1=dvec[:, jj:jj + 1], scalar2=None, op0=ALU.mult),
                        waits=[t_dv, t_uev[jj], tk_yjb[sl]], inc=True)
                    tk_u16_j[jj] = t_acc

                    def emit_exp(n, jj=jj, sl=sl):
                        sd = n % 8
                        return A("scalar", lambda e, jj=jj, n=n, sl=sl, sd=sd: e.activation(
                            out=dab[:, sd * TO:(sd + 1) * TO],
                            in_=dtneg[:, sl * TO:(sl + 1) * TO], func=AF.Exp,
                            scale=apos[:, jj * NST + n:jj * NST + n + 1]),
                            waits=[t_dtn_p[jj], t_ap, tk_da_slot[sd]], inc=True)
                    t_da_n = [None] * (NST + 1)
                    t_da_n[0] = emit_exp(0)
                    t_db = None
                    for n in range(NST):
                        sn = n % 2
                        sd = n % 8
                        s4 = n % 4
                        if n + 1 < NST:
                            t_da_n[n + 1] = emit_exp(n + 1)
                        else:
                            t_da_n[NST] = t_da_n[NST - 1]
                        t_db = A("vector", lambda e, n=n, sl=sl, sn=sn: e.tensor_tensor(
                            out=dbb[:, sn * TO:(sn + 1) * TO],
                            in0=w16b[:, sl * TO:(sl + 1) * TO],
                            in1=bneg[:, n, :], op=ALU.mult),
                            waits=[t_bn[n], t_w16, tk_dbb[sn]], inc=True)
                        idx = jj * NST + n
                        t_scan = A("vector", lambda e, idx=idx, sn=sn, sd=sd, s4=s4: e.tensor_tensor_scan(
                            out=hb[:, s4 * TO:(s4 + 1) * TO],
                            data0=dab[:, sd * TO:(sd + 1) * TO],
                            data1=dbb[:, sn * TO:(sn + 1) * TO],
                            initial=state[:, idx:idx + 1], op0=ALU.mult, op1=ALU.add),
                            waits=[t_da_n[n + 1], t_db, t_ms, tk_hb[s4], prev_state],
                            inc=True)
                        tk_da_slot[sd] = t_scan
                        tk_state_last[0] = A("vector", lambda e, idx=idx, s4=s4: e.tensor_copy(
                            out=state[:, idx:idx + 1],
                            in_=hb[:, (s4 + 1) * TO - 1:(s4 + 1) * TO]),
                            waits=[t_scan], inc=True)
                        t_hc = A("vector", lambda e, n=n, sn=sn, s4=s4: e.tensor_tensor(
                            out=dbb[:, sn * TO:(sn + 1) * TO],
                            in0=hb[:, s4 * TO:(s4 + 1) * TO], in1=cpos[:, n, :],
                            op=ALU.mult), waits=[t_cp[n], t_scan], inc=True)
                        tk_hb[s4] = t_hc
                        t_acc = A("vector", lambda e, sl=sl, sn=sn: e.tensor_tensor(
                            out=yjb[:, sl * TO:(sl + 1) * TO],
                            in0=yjb[:, sl * TO:(sl + 1) * TO],
                            in1=dbb[:, sn * TO:(sn + 1) * TO], op=ALU.add),
                            waits=[t_hc, t_acc], inc=True)
                        tk_dbb[sn] = t_acc
                    tk_w16[sl] = t_db
                    tk_dtn_act[sl] = t_da_n[NST - 1]
                    t_ymix[jj] = A("vector", lambda e, jj=jj, sl=sl: e.tensor_tensor(
                        out=ymix[:, jj, :], in0=yjb[:, sl * TO:(sl + 1) * TO],
                        in1=szb[:, sl * TO:(sl + 1) * TO], op=ALU.mult),
                        waits=[t_sz_p[jj], t_acc] + (tk_ymix_free if jj <= 1 else []),
                        inc=True)
                    tk_sz_slot[sl] = t_ymix[jj]
                    tk_yjb[sl] = t_ymix[jj]
                t_dtp = t_dtp_p[pair[-1]]
                hk = (hooks or {}).get(j0 // 2)
                if hk:
                    hk()
            tk_dtl_free[:] = [t_dtp]
            tk_bcpos_free[:] = [t_ymix[NJ - 1]]
            s["t_ymix"] = t_ymix
            if c + 1 < NCH:
                for _j in range(12, NJ):
                    front_b(c + 1, _j)
                front_tail(c + 1)

        def layer_norm(rd_tok, fp32_apply, dst_fn, src_slice_fn, apply_war_fn=None):
            t_sq = [None] * (NI + 1)
            t_st = None

            def emit_sq(o):
                sl = o % 2
                return A("scalar", lambda e, o=o, sl=sl: e.activation(
                    out=sqtmp[:, sl * TO:(sl + 1) * TO], in_=src_slice_fn(o),
                    func=AF.Square),
                    waits=[rd_tok[o]]
                    + ([tk_sq_slot[sl]] if tk_sq_slot[sl] else []), inc=True)
            t_sq[0] = emit_sq(0)
            for o in range(NI):
                sl = o % 2
                if o + 1 < NI:
                    t_sq[o + 1] = emit_sq(o + 1)
                else:
                    t_sq[NI] = t_sq[NI - 1]

                def st_mms(e, o=o, sl=sl):
                    e.matmul(pss0[0:1, :], ones16[:, 0:1], src_slice_fn(o),
                             start=(o == 0), stop=(o == NI - 1))
                    return e.matmul(pss1[0:1, :], ones16[:, 0:1],
                                    sqtmp[:, sl * TO:(sl + 1) * TO],
                                    start=(o == 0), stop=(o == NI - 1))
                t_st = A("tensor", st_mms,
                         waits=[rd_tok[o], t_sq[o + 1], t_o16, t_st]
                         + ([t for t in tk_pss_free if t] if o == 0 else []), inc=True)
                tk_sq_slot[sl] = t_st
            t_mn = A("scalar", lambda e: e.activation(
                out=stt[0:1, 0:TO], in_=pss0[0:1, :], func=AF.Copy, scale=1.0 / DM),
                waits=[t_st] + tk_ln_free, inc=True)
            t_mq = A("scalar", lambda e: e.activation(
                out=stt[0:1, TO:2 * TO], in_=pss1[0:1, :], func=AF.Copy, scale=1.0 / DM),
                waits=[t_st], inc=True)
            tk_pss_free[0], tk_pss_free[1] = t_mn, t_mq
            t_mm = A("vector", lambda e: e.tensor_tensor(
                out=stt[0:1, 2 * TO:3 * TO], in0=stt[0:1, 0:TO], in1=stt[0:1, 0:TO],
                op=ALU.mult), waits=[t_mn], inc=True)
            t_var = A("vector", lambda e: e.tensor_tensor(
                out=stt[0:1, TO:2 * TO], in0=stt[0:1, TO:2 * TO],
                in1=stt[0:1, 2 * TO:3 * TO], op=ALU.subtract),
                waits=[t_mq, t_mm], inc=True)
            t_ln = A("scalar", lambda e: e.activation(
                out=stt[0:1, 2 * TO:3 * TO], in_=stt[0:1, TO:2 * TO], func=AF.Ln,
                bias=epsc[0:1, 0:1]), waits=[t_var, t_eps], inc=True)
            t_rs = A("scalar", lambda e: e.activation(
                out=stt[0:1, 2 * TO:3 * TO], in_=stt[0:1, 2 * TO:3 * TO], func=AF.Exp,
                scale=-0.5), waits=[t_ln], inc=True)
            t_nm = A("vector", lambda e: e.tensor_tensor(
                out=stt[0:1, 0:TO], in0=stt[0:1, 0:TO],
                in1=stt[0:1, 2 * TO:3 * TO], op=ALU.mult), waits=[t_rs], inc=True)
            if fp32_apply:
                t_ng = A("vector", lambda e: e.tensor_scalar_mul(
                    out=stt[0:1, 0:TO], in0=stt[0:1, 0:TO], scalar1=-1.0),
                    waits=[t_nm], inc=True)
                rsrc, bsrc = stt[0:1, 2 * TO:3 * TO], stt[0:1, 0:TO]
                lhs_one = onesr32[0:1, :]
                dst = abuf2
            else:
                A("vector", lambda e: e.tensor_copy(
                    out=stt16[0:1, 0:TO], in_=stt[0:1, 2 * TO:3 * TO]), waits=[t_rs])
                t_ng = A("vector", lambda e: e.tensor_scalar(
                    out=stt16[0:1, TO:2 * TO], in0=stt[0:1, 0:TO],
                    scalar1=-1.0, scalar2=None, op0=ALU.mult), waits=[t_nm], inc=True)
                rsrc, bsrc = stt16[0:1, 0:TO], stt16[0:1, TO:2 * TO]
                lhs_one = onesr16[0:1, :]
                dst = abuf
            toks = []
            for which, srcv in ((0, rsrc), (1, bsrc)):
                r, fr = rot_acquire()
                tm = A("tensor", lambda e, srcv=srcv, r=r, lhs_one=lhs_one: e.matmul(
                    psr[r][:, :], lhs_one, srcv, start=True, stop=True),
                    waits=[t_ng, fr, t_or16], inc=True)
                tv = A("vector", lambda e, which=which, r=r, dst=dst: e.tensor_copy(
                    out=dst[:, which * TO:(which + 1) * TO], in_=psr[r][:, :]),
                    waits=[tm], inc=True)
                rot_free[r] = tv
                toks.append(tv)
                tk_ln_free[0] = tm
            if dst_fn is None:
                return toks, dst
            t_apply = [None] * NI
            for o in range(NI):
                war = [apply_war_fn(o)] if apply_war_fn else []
                t_a1 = A("vector", lambda e, o=o, dst=dst: e.tensor_tensor(
                    out=dst_fn(o), in0=src_slice_fn(o),
                    in1=dst[:, 0:TO], op=ALU.mult),
                    waits=[rd_tok[o]] + toks + war, inc=True)
                t_apply[o] = A("vector", lambda e, o=o, dst=dst: e.tensor_tensor(
                    out=dst_fn(o), in0=dst_fn(o),
                    in1=dst[:, TO:2 * TO], op=ALU.add),
                    waits=[t_a1], inc=True)
            return t_apply

        def back0(c):
            pc = c % 2
            xo = pc * NI
            s = st[c]
            t_ymix = s["t_ymix"]
            t_r1 = [None] * NI
            t_op = None
            for o in range(NI):
                sl = o % 2
                tdw = A("sync", lambda e, o=o, sl=sl: e.dma_start(
                    out=wobuf[:, sl * NJ * P:(sl + 1) * NJ * P], in_=wo_ld.ap()[o]),
                    waits=[tk_wo_slot[sl]], inc=True, sem="wo0" if sl == 0 else "wo1")
                r, fr = rot_acquire()

                def op_mms(e, o=o, sl=sl, r=r):
                    last = None
                    for j in range(NJ):
                        last = e.matmul(psr[r][:, :],
                                        wobuf[:, (sl * NJ + j) * P:(sl * NJ + j + 1) * P],
                                        ymix[:, j, :], start=(j == 0), stop=(j == NJ - 1))
                    return last
                t_op = A("tensor", op_mms, waits=[tdw, fr, t_ymix[NJ - 1]], inc=True)
                tk_wo_slot[sl] = t_op
                t_r1[o] = A("vector", lambda e, o=o, r=r, xo=xo: e.tensor_tensor(
                    out=x16[:, xo + o, 3:3 + TO], in0=x16[:, xo + o, 3:3 + TO],
                    in1=psr[r][:, :], op=ALU.add), waits=[t_op], inc=True)
                rot_free[r] = t_r1[o]
            tk_ymix_free[:] = [t_op]
            s["t_r1"] = t_r1

        def back1(c):
            pc = c % 2
            xo = pc * NI
            s = st[c]
            t_r1 = s["t_r1"]
            t_x1 = layer_norm(t_r1, False,
                              dst_fn=lambda o: x1b[:, o, :],
                              src_slice_fn=lambda o, xo=xo: x16[:, xo + o, 3:3 + TO],
                              apply_war_fn=lambda o: tk_x1_free[0])
            s["t_x1"] = t_x1

        def back2(c):
            pc = c % 2
            xo = pc * NI
            s = st[c]
            t_x1 = s["t_x1"]
            t_r2 = [None] * NI
            t_ff = None
            for o in range(NI):
                sl = o % 2
                tdwf = A("sync", lambda e, o=o, sl=sl: e.dma_start(
                    out=wfbuf[:, sl * NI * P:(sl + 1) * NI * P], in_=wf_ld.ap()[o]),
                    waits=[tk_wf_slot[sl]], inc=True, sem="wf0" if sl == 0 else "wf1")
                r, fr = rot_acquire()

                def ff_mms(e, o=o, sl=sl, r=r):
                    last = None
                    for i in range(NI):
                        last = e.matmul(psr[r][:, :],
                                        wfbuf[:, (sl * NI + i) * P:(sl * NI + i + 1) * P],
                                        x1b[:, i, :], start=(i == 0), stop=(i == NI - 1))
                    return last
                t_ff = A("tensor", ff_mms, waits=[tdwf, fr, t_x1[NI - 1]], inc=True)
                tk_wf_slot[sl] = t_ff
                t_r2[o] = A("vector", lambda e, o=o, r=r, xo=xo: e.tensor_tensor(
                    out=x16[:, xo + o, 3:3 + TO], in0=x1b[:, o, :], in1=psr[r][:, :],
                    op=ALU.add), waits=[t_ff], inc=True)
                rot_free[r] = t_r2[o]
            tk_x1_free[:] = [t_ff]
            s["t_r2"] = t_r2

        def back3(c):
            pc = c % 2
            xo = pc * NI
            s = st[c]
            t_r2 = s["t_r2"]
            toks2, dst2 = layer_norm(t_r2, True, dst_fn=None,
                                     src_slice_fn=lambda o, xo=xo: x16[:, xo + o, 3:3 + TO])
            t_tp_last_per_tt = [None] * 4
            t_o2_last = None
            t0 = c * TO
            for o in range(NI):
                sl = o % 2
                t_oa = A("vector", lambda e, o=o, sl=sl, xo=xo: e.tensor_tensor(
                    out=outf[:, sl * TO:(sl + 1) * TO],
                    in0=x16[:, xo + o, 3:3 + TO], in1=dst2[:, 0:TO],
                    op=ALU.mult),
                    waits=[t_r2[o]] + toks2
                    + ([tk_outf_slot[sl]] if tk_outf_slot[sl] else []), inc=True)
                t_o2_last = A("vector", lambda e, sl=sl: e.tensor_tensor(
                    out=outf[:, sl * TO:(sl + 1) * TO],
                    in0=outf[:, sl * TO:(sl + 1) * TO],
                    in1=dst2[:, TO:2 * TO], op=ALU.add),
                    waits=[t_oa], inc=True)
                for tt in range(4):
                    r, fr = rot_acquire()
                    tm = A("tensor", lambda e, tt=tt, r=r, sl=sl: e.transpose(
                        psr[r][:, :P], outf[:, sl * TO + tt * P:sl * TO + (tt + 1) * P],
                        ident[:, :]), waits=[t_o2_last, fr], inc=True)
                    ta = A("scalar", lambda e, o=o, tt=tt, r=r: e.activation(
                        out=otok[:, tt * DM + o * P:tt * DM + (o + 1) * P],
                        in_=psr[r][:, :P], func=AF.Copy),
                        waits=[tm] + ([tk_ot_slot[tt]] if tk_ot_slot[tt] else []), inc=True)
                    rot_free[r] = ta
                    t_tp_last_per_tt[tt] = ta
                    tk_outf_slot[sl] = tm
            for tt in range(4):
                tdo = A("sync", lambda e, t0=t0, tt=tt: e.dma_start(
                    out=yout[t0 + tt * P:t0 + (tt + 1) * P, :],
                    in_=otok[:, tt * DM:(tt + 1) * DM]),
                    waits=[t_tp_last_per_tt[tt]], inc=True, sem="ot%d" % tt)
                tk_ot_slot[tt] = tdo
            tk_x16_pp[pc] = [t_o2_last, tk_sq_slot[0], tk_sq_slot[1]]

        front_a(0)
        for j in range(NJ):
            front_b(0, j)
        front_tail(0)
        for c in range(NCH):
            if c > 0:
                back0(c - 1)
            hooks = {}
            if c > 0:
                hooks[1] = lambda cc=c - 1: back1(cc)
                hooks[2] = lambda cc=c - 1: back2(cc)
                hooks[3] = lambda cc=c - 1: back3(cc)
            if c + 1 < NCH:
                hooks[4] = lambda cc=c + 1: front_a(cc)
                hooks[5] = lambda cc=c + 1: [front_b(cc, _j) for _j in range(6)]
                hooks[6] = lambda cc=c + 1: [front_b(cc, _j) for _j in range(6, 12)]
            scan(c, hooks)
        back0(NCH - 1)
        back1(NCH - 1)
        back2(NCH - 1)
        back3(NCH - 1)

        finals = [(s, pr.counts[s]) for s in ("sync",) + Prog.CHANNELS
                  if pr.counts[s] > 0]

        def final_waits(e):
            for s, v in finals:
                e.wait_ge(pr.sems[s], v)
        A("sync", final_waits)

        block = ent(nc.Block())
        pr.emit(block)
    return nc


_NC_CACHE = [None]


def kernel(L0_attn_x, L0_attn_W_in, L0_attn_conv_k, L0_attn_conv_b,
           L0_attn_W_xproj, L0_attn_W_dt, L0_attn_b_dt, L0_attn_A_log,
           L0_attn_D, L0_attn_W_out, L0_ffn_weight):
    f32 = np.float32
    x = np.asarray(L0_attn_x, f32)
    W_in = np.ascontiguousarray(np.asarray(L0_attn_W_in, f32))
    ck = np.ascontiguousarray(np.asarray(L0_attn_conv_k, f32)[:, 0, :])
    cb = np.ascontiguousarray(np.asarray(L0_attn_conv_b, f32).reshape(NJ, P).T)
    wxp = np.ascontiguousarray(
        np.asarray(L0_attn_W_xproj, f32).reshape(NJ, P, 160).transpose(1, 0, 2).reshape(P, NJ * 160))
    wdt = np.ascontiguousarray(np.asarray(L0_attn_W_dt, f32))
    bdt = np.ascontiguousarray(np.asarray(L0_attn_b_dt, f32).reshape(NJ, P).T)
    ap_ = np.ascontiguousarray(
        np.asarray(L0_attn_A_log, f32).reshape(NJ, P, NST).transpose(1, 0, 2).reshape(P, NJ * NST))
    dp = np.ascontiguousarray(np.asarray(L0_attn_D, f32).reshape(NJ, P).T)
    wout = np.ascontiguousarray(np.asarray(L0_attn_W_out, f32))
    wffn = np.ascontiguousarray(np.asarray(L0_ffn_weight, f32))
    ident = np.eye(P, dtype=f32)
    onescol = np.zeros((P, 2), f32)
    onescol[:, 0] = 1.0
    onesrow = np.ones((1, P), f32)
    selc = np.zeros((32, 2 * DI), f32)
    for n in range(NST):
        selc[n, n * P:(n + 1) * P] = -1.0
        selc[NST + n, DI + n * P:DI + (n + 1) * P] = 1.0

    if _NC_CACHE[0] is None:
        _NC_CACHE[0] = build_nc()
    nc = _NC_CACHE[0]

    shared = dict(w_in=W_in, conv_k=ck, cb_perm=cb, wxp_perm=wxp, w_dt=wdt,
                  bdt_perm=bdt, a_perm=ap_, d_perm=dp, w_out=wout, w_ffn=wffn,
                  identc=ident, onescol=onescol, onesrow=onesrow, selc=selc)
    in_maps = [dict(shared, xin=np.ascontiguousarray(x[b])) for b in range(8)]
    res = run_bass_kernel_spmd(nc, in_maps, core_ids=list(range(8)))
    out = np.stack([np.asarray(res.results[b]["yout"]) for b in range(8)], axis=0)
    return out


# version 25
# speedup vs baseline: 1.0160x; 1.0160x over previous
import contextlib
import numpy as np

import concourse.bass as bass
import concourse.mybir as mybir
from concourse.bass_utils import run_bass_kernel_spmd

F32 = mybir.dt.float32
BF16 = mybir.dt.bfloat16
AF = mybir.ActivationFunctionType
ALU = mybir.AluOpType

P = 128
L = 4096
DM = 1024
DI = 2048
NST = 16
TO = 512
NCH = L // TO
NJ = DI // P
NI = DM // P


class Prog:

    ENGS = ("tensor", "vector", "scalar", "sync", "gpsimd")
    CHANNELS = ("wt0", "wt1", "w2a", "w2b", "wo0", "wo1", "wf0", "wf1",
                "xs", "ot0", "ot1", "ot2", "ot3")

    def __init__(self, nc, ctx):
        self.nc = nc
        self.ops = {e: [] for e in self.ENGS}
        self.sems = {e: ctx.enter_context(nc.semaphore("sem_" + e)) for e in self.ENGS}
        for ch in self.CHANNELS:
            self.sems[ch] = ctx.enter_context(nc.semaphore("sem_" + ch))
        self.counts = {s: 0 for s in self.sems}

    def add(self, eng, fn, waits=(), inc=False, inck=None, sem=None):
        tok = None
        s = sem if sem is not None else eng
        k = inck if inck is not None else (16 if eng == "sync" else 1)
        if inc:
            self.counts[s] += k
            tok = (s, self.counts[s])
        ws = []
        for w in waits:
            if w is None:
                continue
            if isinstance(w, list):
                ws.extend(x for x in w if x is not None)
            else:
                ws.append(w)
        self.ops[eng].append((ws, fn, (s, k) if inc else None))
        return tok

    def emit(self, block):
        for eng in self.ENGS:
            ops = self.ops[eng]
            if not ops:
                continue
            sems = self.sems

            def make(ops, eng):
                def body(e):
                    seen = {}
                    for ws, fn, incsk in ops:
                        for (psem, v) in ws:
                            if seen.get(psem, 0) >= v:
                                continue
                            seen[psem] = v
                            e.wait_ge(sems[psem], v)
                        last = fn(e)
                        if incsk is not None:
                            s, k = incsk
                            if last is None:
                                last = e.engine_nop() if eng in ("vector", "gpsimd") else e.nop()
                            last.then_inc(sems[s], k)
                return body
            getattr(block, eng)(make(ops, eng))


def build_nc():
    nc = bass.Bass("TRN2", target_bir_lowering=False)

    xin = nc.dram_tensor("xin", [L, DM], F32, kind="ExternalInput").ap()
    w_in = nc.dram_tensor("w_in", [DM, 2 * DI], F32, kind="ExternalInput").ap()
    conv_k = nc.dram_tensor("conv_k", [4, DI], F32, kind="ExternalInput").ap()
    cb_perm = nc.dram_tensor("cb_perm", [P, NJ], F32, kind="ExternalInput").ap()
    wxp_perm = nc.dram_tensor("wxp_perm", [P, NJ * 160], F32, kind="ExternalInput").ap()
    w_dt = nc.dram_tensor("w_dt", [P, DI], F32, kind="ExternalInput").ap()
    bdt_perm = nc.dram_tensor("bdt_perm", [P, NJ], F32, kind="ExternalInput").ap()
    a_perm = nc.dram_tensor("a_perm", [P, NJ * NST], F32, kind="ExternalInput").ap()
    d_perm = nc.dram_tensor("d_perm", [P, NJ], F32, kind="ExternalInput").ap()
    w_out = nc.dram_tensor("w_out", [DI, DM], F32, kind="ExternalInput").ap()
    w_ffn = nc.dram_tensor("w_ffn", [DM, DM], F32, kind="ExternalInput").ap()
    identc = nc.dram_tensor("identc", [P, P], F32, kind="ExternalInput").ap()
    onescol = nc.dram_tensor("onescol", [P, 2], F32, kind="ExternalInput").ap()
    onesrow = nc.dram_tensor("onesrow", [1, P], F32, kind="ExternalInput").ap()
    selc = nc.dram_tensor("selc", [32, 2 * DI], F32, kind="ExternalInput").ap()
    yout = nc.dram_tensor("yout", [L, DM], F32, kind="ExternalOutput").ap()

    wt_ld = nc.dram_tensor("wt_ld", [NJ, P, 32 * P], BF16)
    w2_ld = nc.dram_tensor("w2_ld", [NJ, P, NI * P], BF16)
    wo_ld = nc.dram_tensor("wo_ld", [NI, P, NJ * P], BF16)
    wf_ld = nc.dram_tensor("wf_ld", [NI, P, NI * P], BF16)

    with contextlib.ExitStack() as ctx:
        ent = ctx.enter_context
        pr = Prog(nc, ctx)
        A = pr.add
        p0ch = {}

        def SD(fn, waits=(), ch="sync"):
            t = pr.add("sync", fn, waits=list(waits) + [p0ch.get(ch)],
                       inc=True, sem=ch if ch != "sync" else None)
            p0ch[ch] = t
            return t

        psr = [ent(nc.psum_tensor(f"psr{i}", [P, TO], F32)) for i in range(4)]
        psx0 = ent(nc.psum_tensor("psx0", [P, TO], F32))
        psx1 = ent(nc.psum_tensor("psx1", [32, TO], F32))
        pss0 = ent(nc.psum_tensor("pss0", [1, TO], F32))
        pss1 = ent(nc.psum_tensor("pss1", [1, TO], F32))
        rot_free = [None] * 4
        rot_i = [0]

        def rot_acquire():
            r = rot_i[0] % 4
            rot_i[0] += 1
            return r, rot_free[r]

        ident = ent(nc.sbuf_tensor("ident", [P, P], F32))
        ones32 = ent(nc.sbuf_tensor("ones32", [P, 2], F32))
        ones16 = ent(nc.sbuf_tensor("ones16", [P, 2], BF16))
        onesr32 = ent(nc.sbuf_tensor("onesr32", [1, P], F32))
        onesr16 = ent(nc.sbuf_tensor("onesr16", [1, P], BF16))
        sel16 = ent(nc.sbuf_tensor("sel16", [32, 2 * DI], BF16))
        apos = ent(nc.sbuf_tensor("apos", [P, NJ * NST], F32))
        cbias = ent(nc.sbuf_tensor("cbias", [P, NJ], F32))
        nbdt = ent(nc.sbuf_tensor("nbdt", [P, NJ], F32))
        dvec = ent(nc.sbuf_tensor("dvec", [P, NJ], F32))
        wxp16 = ent(nc.sbuf_tensor("wxp16", [P, NJ * 160], BF16))
        wdt16 = ent(nc.sbuf_tensor("wdt16", [P, DI], BF16))

        t_id = SD(lambda e: e.dma_start(out=ident[:, :], in_=identc), ch="ot0")
        t_oc = SD(lambda e: e.dma_start(out=ones32[:, :], in_=onescol), ch="ot1")
        t_or = SD(lambda e: e.dma_start(out=onesr32[:, :], in_=onesrow), ch="ot2")
        t_o16 = A("vector", lambda e: e.tensor_copy(out=ones16[:, :], in_=ones32[:, :]), waits=[t_oc], inc=True)
        t_or16 = A("vector", lambda e: e.tensor_copy(out=onesr16[:, :], in_=onesr32[:, :]),
                   waits=[t_or], inc=True)
        t = SD(lambda e: e.dma_start(out=apos[:, :], in_=a_perm), ch="ot3")
        t_ap = A("scalar", lambda e: e.activation(out=apos[:, :], in_=apos[:, :], func=AF.Exp), waits=[t], inc=True)
        t_cb = SD(lambda e: e.dma_start(out=cbias[:, :], in_=cb_perm), ch="xs")
        t = SD(lambda e: e.dma_start(out=nbdt[:, :], in_=bdt_perm), ch="ot0")
        t_nb = A("vector", lambda e: e.tensor_scalar_mul(out=nbdt[:, :], in0=nbdt[:, :],
                                                         scalar1=-1.0), waits=[t], inc=True)
        t_dv = SD(lambda e: e.dma_start(out=dvec[:, :], in_=d_perm), ch="ot1")

        with nc.sbuf_tensor("p0sel", [32, 2 * DI], F32) as p0sel, \
             nc.sbuf_tensor("p0big", [P, 4096], F32) as p0big, \
             nc.sbuf_tensor("p0kb", [P, 4 * DI], F32) as p0kb, \
             nc.sbuf_tensor("p0ck", [1, DI], F32) as p0ck, \
             nc.sbuf_tensor("p0wst", [P, 16 * P], F32) as p0wst, \
             nc.sbuf_tensor("p0wj", [P, 4 * 32 * P], BF16) as p0wj, \
             nc.sbuf_tensor("p0w2", [P, 4 * NI * P], BF16) as p0w2, \
             nc.sbuf_tensor("p0wo", [P, DM], BF16) as p0wo:

            t = SD(lambda e: e.dma_start(out=p0sel[:, :], in_=selc), ch="ot2")
            t_sel = A("vector", lambda e: e.tensor_copy(out=sel16[:, :], in_=p0sel[:, :]), waits=[t], inc=True)
            t = SD(lambda e: e.dma_start(out=p0big[:, :NJ * 160], in_=wxp_perm), ch="ot3")
            t_wxp = A("vector", lambda e: e.tensor_copy(out=wxp16[:, :], in_=p0big[:, :NJ * 160]), waits=[t], inc=True)
            t = SD(lambda e: e.dma_start(out=p0big[:, :DI], in_=w_dt),
                  waits=[t_wxp], ch="ot3")
            t_wdt = A("vector", lambda e: e.tensor_copy(out=wdt16[:, :], in_=p0big[:, :DI]), waits=[t], inc=True)

            tprev_ck = t_wdt
            tv = None
            for k in range(4):
                tck = SD(lambda e, k=k: e.dma_start(out=p0ck[:, :], in_=conv_k[k:k + 1, :]),
                        waits=[tprev_ck])
                for q in range(4):
                    r, fr = rot_acquire()
                    tm = A("tensor", lambda e, k=k, q=q, r=r: e.matmul(
                        psr[r][:, :], onesr32[0:1, :], p0ck[0:1, q * TO:(q + 1) * TO],
                        start=True, stop=True), waits=[tck, fr, t_or], inc=True)
                    tv = A("vector", lambda e, k=k, q=q, r=r: e.tensor_copy(
                        out=p0kb[:, k * DI + q * TO:k * DI + (q + 1) * TO], in_=psr[r][:, :]),
                        waits=[tm], inc=True)
                    rot_free[r] = tv
                tprev_ck = tv
            t_kb = tv

            wt_wr = [None] * NJ
            w2_wr = [None] * NJ
            t_wj_dma = None
            t_w2_dma = None
            last_tt = None
            last_c2 = None
            Q = 4
            for j4 in range(0, NJ, Q):
                ltt = [None] * Q
                lc2 = [None] * Q
                for i in range(NI):
                    sl = i % 2
                    td = SD(lambda e, j4=j4, i=i, sl=sl: e.dma_start(
                        out=p0wst[:, sl * Q * P:(sl + 1) * Q * P],
                        in_=w_in[i * P:(i + 1) * P, j4 * P:(j4 + Q) * P]),
                        waits=[last_tt if i >= 2 else None, t_wj_dma],
                        ch="wt1" if sl == 0 else "w2b")
                    for jq in range(Q):
                        j = j4 + jq
                        for k in range(4):
                            last_tt = A("vector", lambda e, jq=jq, j=j, i=i, k=k, sl=sl: e.tensor_tensor(
                                out=p0wj[:, (jq * 32 + i * 4 + k) * P:(jq * 32 + i * 4 + k + 1) * P],
                                in0=p0wst[:, (sl * Q + jq) * P:(sl * Q + jq + 1) * P],
                                in1=p0kb[:, k * DI + j * P:k * DI + (j + 1) * P], op=ALU.mult),
                                waits=[td, t_kb], inc=True)
                        ltt[jq] = last_tt
                    sl2 = i % 2 + 2
                    td2 = SD(lambda e, j4=j4, i=i, sl2=sl2: e.dma_start(
                        out=p0wst[:, sl2 * Q * P:(sl2 + 1) * Q * P],
                        in_=w_in[i * P:(i + 1) * P, DI + j4 * P:DI + (j4 + Q) * P]),
                        waits=[last_c2 if i >= 2 else None, t_w2_dma],
                        ch="wo1" if sl2 == 2 else "wf1")
                    for jq in range(Q):
                        last_c2 = A("vector", lambda e, jq=jq, i=i, sl2=sl2: e.tensor_copy(
                            out=p0w2[:, (jq * NI + i) * P:(jq * NI + i + 1) * P],
                            in_=p0wst[:, (sl2 * Q + jq) * P:(sl2 * Q + jq + 1) * P]),
                            waits=[td2], inc=True)
                        lc2[jq] = last_c2
                for jq in range(Q):
                    j = j4 + jq
                    t_wj_dma = SD(lambda e, j=j, jq=jq: e.dma_start(
                        out=wt_ld.ap()[j], in_=p0wj[:, jq * 32 * P:(jq + 1) * 32 * P]),
                        waits=[ltt[jq]], ch="wt0")
                    t_w2_dma = SD(lambda e, j=j, jq=jq: e.dma_start(
                        out=w2_ld.ap()[j], in_=p0w2[:, jq * NI * P:(jq + 1) * NI * P]),
                        waits=[lc2[jq]], ch="w2a")
                    wt_wr[j] = t_wj_dma
                    w2_wr[j] = t_w2_dma

            t_wo_dma = None
            tcast = None
            for j in range(NJ):
                td = SD(lambda e, j=j: e.dma_start(
                    out=p0big[:, :DM], in_=w_out[j * P:(j + 1) * P, :]),
                    waits=[tcast, t_wxp, t_wo_dma,
                           t_wdt if j == 0 else None],
                    ch="ot0" if j % 2 == 0 else "ot1")
                tcast = A("vector", lambda e: e.tensor_copy(out=p0wo[:, :], in_=p0big[:, :DM]),
                          waits=[td, t_wo_dma], inc=True)
                for o in range(NI):
                    t_wo_dma = SD(lambda e, j=j, o=o: e.dma_start(
                        out=wo_ld.ap()[o][:, j * P:(j + 1) * P],
                        in_=p0wo[:, o * P:(o + 1) * P]), waits=[tcast], ch="wo0")
            wo_wr = t_wo_dma

            t_sgn = None
            t_wf_dma = None
            for ot in range(NI):
                td = SD(lambda e, ot=ot: e.dma_start(
                    out=p0big[:, :DM], in_=w_ffn[ot * P:(ot + 1) * P, :]),
                    waits=[t_sgn, tcast if ot == 0 else None,
                           t_wo_dma if ot == 0 else None],
                    ch="ot2" if ot % 2 == 0 else "ot3")
                for ib in range(NI):
                    r, fr = rot_acquire()
                    tm = A("tensor", lambda e, ib=ib, r=r: e.transpose(
                        psr[r][:, :P], p0big[:, ib * P:(ib + 1) * P], ident[:, :]),
                        waits=[td, fr, t_id], inc=True)
                    t_sgn = A("scalar", lambda e, ib=ib, r=r: e.activation(
                        out=p0wo[:, ib * P:(ib + 1) * P],
                        in_=psr[r][:, :P], func=AF.Sign),
                        waits=[tm, t_wf_dma], inc=True)
                    rot_free[r] = t_sgn
                t_wf_dma = SD(lambda e, ot=ot: e.dma_start(
                    out=wf_ld.ap()[ot], in_=p0wo[:, :]), waits=[t_sgn], ch="wf0")
            wf_wr = t_wf_dma

        wtbuf = ent(nc.sbuf_tensor("wtbuf", [P, 2 * 32 * P], BF16))
        w2buf = ent(nc.sbuf_tensor("w2buf", [P, 2 * NI * P], BF16))
        wobuf = ent(nc.sbuf_tensor("wobuf", [P, 2 * NJ * P], BF16))
        wfbuf = ent(nc.sbuf_tensor("wfbuf", [P, 2 * NI * P], BF16))
        xstag = ent(nc.sbuf_tensor("xstag", [P, DM], F32))
        x16 = ent(nc.sbuf_tensor("x16", [P, 2 * NI, TO + 3], BF16))
        u16 = ent(nc.sbuf_tensor("u16", [P, NJ, TO], BF16))
        dtl16 = ent(nc.sbuf_tensor("dtl16", [P, TO], BF16))
        bc16 = ent(nc.sbuf_tensor("bc16", [32, TO], BF16))
        bneg = ent(nc.sbuf_tensor("bneg", [P, NST, TO], BF16))
        cpos = ent(nc.sbuf_tensor("cpos", [P, NST, TO], BF16))
        dtneg = ent(nc.sbuf_tensor("dtneg", [P, 2 * TO], F32))
        szb = ent(nc.sbuf_tensor("szb", [P, 2 * TO], BF16))
        w16b = ent(nc.sbuf_tensor("w16b", [P, 2 * TO], BF16))
        dab = ent(nc.sbuf_tensor("dab", [P, 8 * TO], BF16))
        dbb = ent(nc.sbuf_tensor("dbb", [P, 2 * TO], BF16))
        hb = ent(nc.sbuf_tensor("hb", [P, 4 * TO], BF16))
        yjb = ent(nc.sbuf_tensor("yjb", [P, 2 * TO], BF16))
        ymix = ent(nc.sbuf_tensor("ymix", [P, NJ, TO], BF16))
        sqtmp = ent(nc.sbuf_tensor("sqtmp", [P, 2 * TO], BF16))
        x1b = ent(nc.sbuf_tensor("x1b", [P, NI, TO], BF16))
        outf = ent(nc.sbuf_tensor("outf", [P, 2 * TO], F32))
        otok = ent(nc.sbuf_tensor("otok", [P, 4 * DM], F32))
        state = ent(nc.sbuf_tensor("state", [P, NJ * NST], F32))
        stt = ent(nc.sbuf_tensor("stt", [1, 3 * TO], F32))
        epsc = ent(nc.sbuf_tensor("epsc", [1, 2], F32))
        tinyb = ent(nc.sbuf_tensor("tinyb", [P, 2], F32))
        stt16 = ent(nc.sbuf_tensor("stt16", [1, 2 * TO], BF16))
        abuf = ent(nc.sbuf_tensor("abuf", [P, 2 * TO], BF16))
        abuf2 = ent(nc.sbuf_tensor("abuf2", [P, 2 * TO], F32))
        flsh = ent(nc.sbuf_tensor("flsh", [P, 8], F32))

        for _eng in ("vector", "scalar", "tensor", "gpsimd"):
            A(_eng, lambda e: None, waits=list(p0ch.values()), inc=True)
        t_ms = A("vector", lambda e: e.memset(state[:, :], 0.0), inc=True)
        t_eps = A("vector", lambda e: e.memset(epsc[0:1, :], 1e-5), inc=True)
        t_tb = A("vector", lambda e: e.memset(tinyb[:, :], 1e-30), inc=True)
        t_mx = A("vector", lambda e: e.memset(x16[:, :, 0:3], 0.0), inc=True)

        tk_x16_pp = {0: [t_mx], 1: [t_mx]}
        tk_xz_hist = {0: None, 1: None}
        tk_tail = {0: t_mx, 1: t_mx}
        tk_u16_j = [None] * NJ
        tk_uxp = [None]
        tk_dtl_free = [None]
        tk_bc_free = [None]
        tk_bcpos_free = [None]
        tk_ymix_free = [None]
        tk_x1_free = [None]
        tk_psx_free = [None]
        tk_pss_free = [None, None]
        tk_wt_slot = [wt_wr[NJ - 1], [wt_wr[NJ - 1], p0ch.get("wt1")]]
        tk_w2_slot = [w2_wr[NJ - 1], [w2_wr[NJ - 1], p0ch.get("w2b")]]
        tk_wo_slot = [wo_wr, [wo_wr, p0ch.get("wo1")]]
        tk_wf_slot = [wf_wr, [wf_wr, p0ch.get("wf1")]]
        tk_xs_slot = [p0ch.get("xs")]
        tk_ot_slot = [p0ch.get("ot0"), p0ch.get("ot1"), p0ch.get("ot2"), p0ch.get("ot3")]
        tk_sz_slot = [None] * 2
        tk_dtn_slot = [None] * 2
        tk_da_slot = [None] * 8
        tk_sq_slot = [None, None]
        tk_outf_slot = [None] * 2
        tk_ln_free = [None]
        tk_dbb = [None, None]
        tk_hb = [None] * 4
        tk_yjb = [None, None]
        tk_w16 = [None, None]
        tk_dtn_act = [None, None]
        tk_state_last = [None]

        st = {}
        BARRIER = [False]

        def hard_sync():
            if not BARRIER[0]:
                return
            toks = [(e, pr.counts[e]) for e in ("tensor", "vector", "scalar")
                    if pr.counts[e] > 0]
            for e in ("tensor", "vector", "scalar"):
                A(e, lambda e2: None, waits=toks, inc=True)

        def front_a(c):
            t0 = c * TO
            pc = c % 2
            xo = pc * NI
            s = st[c] = {}
            ta_last = None
            for tt in range(4):
                td = A("sync", lambda e, t0=t0, tt=tt: e.dma_start(
                    out=xstag[:, :], in_=xin[t0 + tt * P:t0 + (tt + 1) * P, :]),
                    waits=[tk_xs_slot[0]], inc=True, sem="xs")
                tm = None
                for ib in range(NI):
                    r, fr = rot_acquire()
                    tm = A("tensor", lambda e, ib=ib, r=r: e.transpose(
                        psr[r][:, :P], xstag[:, ib * P:(ib + 1) * P], ident[:, :]),
                        waits=[td, fr] + tk_x16_pp[pc], inc=True)
                    ta_last = A("scalar", lambda e, tt=tt, ib=ib, r=r, xo=xo: e.activation(
                        out=x16[:, xo + ib, 3 + tt * P:3 + (tt + 1) * P], in_=psr[r][:, :P],
                        func=AF.Copy), waits=[tm] + tk_x16_pp[pc], inc=True)
                    rot_free[r] = ta_last
                tk_xs_slot[0] = tm
            s["ta_last"] = ta_last
            s["t_uev"] = [None] * NJ
            s["t_xp"] = None
            s["t_xz_last"] = None

        def front_b(c, j):
            pc = c % 2
            xo = pc * NI
            s = st[c]
            ta_last = s["ta_last"]
            sl = j % 2
            tdw = A("sync", lambda e, j=j, sl=sl: e.dma_start(
                out=wtbuf[:, sl * 32 * P:(sl + 1) * 32 * P], in_=wt_ld.ap()[j]),
                waits=[tk_wt_slot[sl]], inc=True, sem="wt0" if sl == 0 else "wt1")
            r, fr = rot_acquire()

            def xz_mms(e, j=j, sl=sl, r=r, xo=xo):
                last = None
                for i in range(NI):
                    for k in range(4):
                        last = e.matmul(
                            psr[r][:, :],
                            wtbuf[:, (sl * 32 + i * 4 + k) * P:(sl * 32 + i * 4 + k + 1) * P],
                            x16[:, xo + i, k:k + TO],
                            start=(i == 0 and k == 0), stop=(i == NI - 1 and k == 3))
                return last
            t_xz_last = A("tensor", xz_mms, waits=[tdw, fr, ta_last, tk_tail[pc]],
                          inc=True)
            tk_wt_slot[sl] = t_xz_last
            s["t_xz_last"] = t_xz_last
            prev_xp = st[c - 1]["t_xp"] if c > 0 else None
            s["t_uev"][j] = A("scalar", lambda e, j=j, r=r: e.activation(
                out=u16[:, j, :], in_=psr[r][:, :], func=AF.Silu,
                bias=cbias[:, j:j + 1], scale=1.0),
                waits=[t_xz_last, t_cb, tk_u16_j[j], prev_xp], inc=True)
            rot_free[r] = s["t_uev"][j]


        def front_tail(c):
            pc = c % 2
            xo = pc * NI
            s = st[c]
            ta_last = s["ta_last"]

            def xp_mms(e):
                last = None
                for j in range(NJ):
                    e.matmul(psx0[:, :], wxp16[:, j * 160:j * 160 + P], u16[:, j, :],
                             start=(j == 0), stop=(j == NJ - 1))
                    last = e.matmul(psx1[:, :], wxp16[:, j * 160 + P:j * 160 + 160],
                                    u16[:, j, :], start=(j == 0), stop=(j == NJ - 1))
                return last
            t_xp = A("tensor", xp_mms,
                     waits=[s["t_uev"][NJ - 1], t_wxp]
                     + [t for t in tk_psx_free if t], inc=True)
            s["t_xp"] = t_xp
            tk_tail[1 - pc] = A("vector", lambda e, xo=xo, pc=pc: e.tensor_copy(
                out=x16[:, (1 - pc) * NI:(1 - pc) * NI + NI, 0:3],
                in_=x16[:, xo:xo + NI, TO:TO + 3]),
                waits=[ta_last, tk_xz_hist[1 - pc]], inc=True)
            tk_xz_hist[pc] = s["t_xz_last"]
            t_dtl = A("vector", lambda e: e.tensor_copy(out=dtl16[:, :], in_=psx0[:, :]),
                      waits=[t_xp] + tk_dtl_free, inc=True)
            t_bc = A("vector", lambda e: e.tensor_copy(out=bc16[:, :], in_=psx1[:, :]),
                     waits=[t_xp] + tk_bc_free, inc=True)
            tk_psx_free[:] = [t_dtl, t_bc]
            s["t_dtl"] = t_dtl
            t_bn = [None] * NST
            t_cp = [None] * NST
            t_bcast_pe = None
            for n in range(NST):
                r, fr = rot_acquire()
                t_bcast_pe = A("tensor", lambda e, n=n, r=r: e.matmul(
                    psr[r][:, :], sel16[:, n * P:(n + 1) * P], bc16[:, :],
                    start=True, stop=True), waits=[t_bc, t_sel, fr], inc=True)
                t_bn[n] = A("vector", lambda e, n=n, r=r: e.tensor_copy(
                    out=bneg[:, n, :], in_=psr[r][:, :]),
                    waits=[t_bcast_pe] + tk_bcpos_free, inc=True)
                rot_free[r] = t_bn[n]
                r, fr = rot_acquire()
                t_bcast_pe = A("tensor", lambda e, n=n, r=r: e.matmul(
                    psr[r][:, :], sel16[:, DI + n * P:DI + (n + 1) * P], bc16[:, :],
                    start=True, stop=True), waits=[t_bc, t_sel, fr], inc=True)
                t_cp[n] = A("vector", lambda e, n=n, r=r: e.tensor_copy(
                    out=cpos[:, n, :], in_=psr[r][:, :]),
                    waits=[t_bcast_pe] + tk_bcpos_free, inc=True)
                rot_free[r] = t_cp[n]
            tk_bc_free[:] = [t_bcast_pe]
            s["t_bn"], s["t_cp"] = t_bn, t_cp

        def scan(c, hooks=None):
            pc = c % 2
            xo = pc * NI
            s = st[c]
            t_uev, t_dtl = s["t_uev"], s["t_dtl"]
            t_bn, t_cp = s["t_bn"], s["t_cp"]
            ta_last = s["ta_last"]
            prev_state = tk_state_last[0]
            t_ymix = [None] * NJ
            t_dtp = None
            for j0 in range(0, NJ, 2):
                pair = (j0, j0 + 1)
                t_dtp_p, t_z_p, rr1, rr2 = {}, {}, {}, {}
                for jj in pair:
                    sl = jj % 2
                    r1, fr1 = rot_acquire()
                    t_dtp_p[jj] = A("tensor", lambda e, jj=jj, r1=r1: e.matmul(
                        psr[r1][:, :], wdt16[:, jj * P:(jj + 1) * P], dtl16[:, :],
                        start=True, stop=True), waits=[t_dtl, t_wdt, fr1], inc=True)
                    rr1[jj] = r1
                    tdw2 = A("sync", lambda e, jj=jj, sl=sl: e.dma_start(
                        out=w2buf[:, sl * NI * P:(sl + 1) * NI * P], in_=w2_ld.ap()[jj]),
                        waits=[tk_w2_slot[sl]], inc=True, sem="w2a" if sl == 0 else "w2b")
                    r2, fr2 = rot_acquire()

                    def z_mms(e, jj=jj, sl=sl, r2=r2, xo=xo):
                        last = None
                        for i in range(NI):
                            last = e.matmul(psr[r2][:, :],
                                            w2buf[:, (sl * NI + i) * P:(sl * NI + i + 1) * P],
                                            x16[:, xo + i, 3:3 + TO],
                                            start=(i == 0), stop=(i == NI - 1))
                        return last
                    t_z_p[jj] = A("tensor", z_mms, waits=[tdw2, fr2, ta_last], inc=True)
                    tk_w2_slot[sl] = t_z_p[jj]
                    rr2[jj] = r2
                t_sg_p = {}
                for jj in pair:
                    sl = jj % 2
                    t_sg_p[jj] = A("scalar", lambda e, jj=jj, sl=sl, r1=rr1[jj]: e.activation(
                        out=dtneg[:, sl * TO:(sl + 1) * TO], in_=psr[r1][:, :],
                        func=AF.Sigmoid, scale=-1.0, bias=nbdt[:, jj:jj + 1]),
                        waits=[t_dtp_p[jj], t_nb, tk_dtn_slot[sl], tk_dtn_act[sl]],
                        inc=True)
                    rot_free[rr1[jj]] = t_sg_p[jj]
                t_sz_p = {}
                for jj in pair:
                    sl = jj % 2
                    t_sz_p[jj] = A("scalar", lambda e, sl=sl, r2=rr2[jj]: e.activation(
                        out=szb[:, sl * TO:(sl + 1) * TO], in_=psr[r2][:, :], func=AF.Silu),
                        waits=[t_z_p[jj], tk_sz_slot[sl]], inc=True)
                    rot_free[rr2[jj]] = t_sz_p[jj]
                t_dtn_p = {}
                for jj in pair:
                    sl = jj % 2
                    t_dtn_p[jj] = A("scalar", lambda e, sl=sl: e.activation(
                        out=dtneg[:, sl * TO:(sl + 1) * TO],
                        in_=dtneg[:, sl * TO:(sl + 1) * TO], func=AF.Ln,
                        bias=tinyb[:, 0:1]), waits=[t_tb, t_sg_p[jj]], inc=True)
                for jj in pair:
                    sl = jj % 2
                    t_w16 = A("vector", lambda e, jj=jj, sl=sl: e.tensor_tensor(
                        out=w16b[:, sl * TO:(sl + 1) * TO],
                        in0=dtneg[:, sl * TO:(sl + 1) * TO],
                        in1=u16[:, jj, :], op=ALU.mult),
                        waits=[t_dtn_p[jj], t_uev[jj], tk_w16[sl]], inc=True)
                    tk_dtn_slot[sl] = t_w16
                    t_acc = A("vector", lambda e, jj=jj, sl=sl: e.tensor_scalar(
                        out=yjb[:, sl * TO:(sl + 1) * TO], in0=u16[:, jj, :],
                        scalar1=dvec[:, jj:jj + 1], scalar2=None, op0=ALU.mult),
                        waits=[t_dv, t_uev[jj], tk_yjb[sl]], inc=True)
                    tk_u16_j[jj] = t_acc

                    def emit_exp(n, jj=jj, sl=sl):
                        sd = n % 8
                        return A("scalar", lambda e, jj=jj, n=n, sl=sl, sd=sd: e.activation(
                            out=dab[:, sd * TO:(sd + 1) * TO],
                            in_=dtneg[:, sl * TO:(sl + 1) * TO], func=AF.Exp,
                            scale=apos[:, jj * NST + n:jj * NST + n + 1]),
                            waits=[t_dtn_p[jj], t_ap, tk_da_slot[sd]], inc=True)
                    t_da_n = [None] * (NST + 1)
                    t_da_n[0] = emit_exp(0)
                    t_db = None
                    for n in range(NST):
                        sn = n % 2
                        sd = n % 8
                        s4 = n % 4
                        if n + 1 < NST:
                            t_da_n[n + 1] = emit_exp(n + 1)
                        else:
                            t_da_n[NST] = t_da_n[NST - 1]
                        t_db = A("vector", lambda e, n=n, sl=sl, sn=sn: e.tensor_tensor(
                            out=dbb[:, sn * TO:(sn + 1) * TO],
                            in0=w16b[:, sl * TO:(sl + 1) * TO],
                            in1=bneg[:, n, :], op=ALU.mult),
                            waits=[t_bn[n], t_w16, tk_dbb[sn]], inc=True)
                        idx = jj * NST + n
                        t_scan = A("vector", lambda e, idx=idx, sn=sn, sd=sd, s4=s4: e.tensor_tensor_scan(
                            out=hb[:, s4 * TO:(s4 + 1) * TO],
                            data0=dab[:, sd * TO:(sd + 1) * TO],
                            data1=dbb[:, sn * TO:(sn + 1) * TO],
                            initial=state[:, idx:idx + 1], op0=ALU.mult, op1=ALU.add),
                            waits=[t_da_n[n + 1], t_db, t_ms, tk_hb[s4], prev_state],
                            inc=True)
                        tk_da_slot[sd] = t_scan
                        tk_state_last[0] = A("vector", lambda e, idx=idx, s4=s4: e.tensor_copy(
                            out=state[:, idx:idx + 1],
                            in_=hb[:, (s4 + 1) * TO - 1:(s4 + 1) * TO]),
                            waits=[t_scan], inc=True)
                        t_hc = A("vector", lambda e, n=n, sn=sn, s4=s4: e.tensor_tensor(
                            out=dbb[:, sn * TO:(sn + 1) * TO],
                            in0=hb[:, s4 * TO:(s4 + 1) * TO], in1=cpos[:, n, :],
                            op=ALU.mult), waits=[t_cp[n], t_scan], inc=True)
                        tk_hb[s4] = t_hc
                        t_acc = A("vector", lambda e, sl=sl, sn=sn: e.tensor_tensor(
                            out=yjb[:, sl * TO:(sl + 1) * TO],
                            in0=yjb[:, sl * TO:(sl + 1) * TO],
                            in1=dbb[:, sn * TO:(sn + 1) * TO], op=ALU.add),
                            waits=[t_hc, t_acc], inc=True)
                        tk_dbb[sn] = t_acc
                    tk_w16[sl] = t_db
                    tk_dtn_act[sl] = t_da_n[NST - 1]
                    t_ymix[jj] = A("vector", lambda e, jj=jj, sl=sl: e.tensor_tensor(
                        out=ymix[:, jj, :], in0=yjb[:, sl * TO:(sl + 1) * TO],
                        in1=szb[:, sl * TO:(sl + 1) * TO], op=ALU.mult),
                        waits=[t_sz_p[jj], t_acc] + (tk_ymix_free if jj <= 1 else []),
                        inc=True)
                    tk_sz_slot[sl] = t_ymix[jj]
                    tk_yjb[sl] = t_ymix[jj]
                t_dtp = t_dtp_p[pair[-1]]
                hk = (hooks or {}).get(j0 // 2)
                if hk:
                    hk()
            tk_dtl_free[:] = [t_dtp]
            tk_bcpos_free[:] = [t_ymix[NJ - 1]]
            s["t_ymix"] = t_ymix
            if c + 1 < NCH:
                for _j in range(12, NJ):
                    front_b(c + 1, _j)
                front_tail(c + 1)

        def layer_norm(rd_tok, fp32_apply, dst_fn, src_slice_fn, apply_war_fn=None):
            t_sq = [None] * (NI + 1)
            t_st = None

            def emit_sq(o):
                sl = o % 2
                return A("scalar", lambda e, o=o, sl=sl: e.activation(
                    out=sqtmp[:, sl * TO:(sl + 1) * TO], in_=src_slice_fn(o),
                    func=AF.Square),
                    waits=[rd_tok[o]]
                    + ([tk_sq_slot[sl]] if tk_sq_slot[sl] else []), inc=True)
            t_sq[0] = emit_sq(0)
            for o in range(NI):
                sl = o % 2
                if o + 1 < NI:
                    t_sq[o + 1] = emit_sq(o + 1)
                else:
                    t_sq[NI] = t_sq[NI - 1]

                def st_mms(e, o=o, sl=sl):
                    e.matmul(pss0[0:1, :], ones16[:, 0:1], src_slice_fn(o),
                             start=(o == 0), stop=(o == NI - 1))
                    return e.matmul(pss1[0:1, :], ones16[:, 0:1],
                                    sqtmp[:, sl * TO:(sl + 1) * TO],
                                    start=(o == 0), stop=(o == NI - 1))
                t_st = A("tensor", st_mms,
                         waits=[rd_tok[o], t_sq[o + 1], t_o16, t_st]
                         + ([t for t in tk_pss_free if t] if o == 0 else []), inc=True)
                tk_sq_slot[sl] = t_st
            t_mn = A("scalar", lambda e: e.activation(
                out=stt[0:1, 0:TO], in_=pss0[0:1, :], func=AF.Copy, scale=1.0 / DM),
                waits=[t_st] + tk_ln_free, inc=True)
            t_mq = A("scalar", lambda e: e.activation(
                out=stt[0:1, TO:2 * TO], in_=pss1[0:1, :], func=AF.Copy, scale=1.0 / DM),
                waits=[t_st], inc=True)
            tk_pss_free[0], tk_pss_free[1] = t_mn, t_mq
            t_mm = A("vector", lambda e: e.tensor_tensor(
                out=stt[0:1, 2 * TO:3 * TO], in0=stt[0:1, 0:TO], in1=stt[0:1, 0:TO],
                op=ALU.mult), waits=[t_mn], inc=True)
            t_var = A("vector", lambda e: e.tensor_tensor(
                out=stt[0:1, TO:2 * TO], in0=stt[0:1, TO:2 * TO],
                in1=stt[0:1, 2 * TO:3 * TO], op=ALU.subtract),
                waits=[t_mq, t_mm], inc=True)
            t_ln = A("scalar", lambda e: e.activation(
                out=stt[0:1, 2 * TO:3 * TO], in_=stt[0:1, TO:2 * TO], func=AF.Ln,
                bias=epsc[0:1, 0:1]), waits=[t_var, t_eps], inc=True)
            t_rs = A("scalar", lambda e: e.activation(
                out=stt[0:1, 2 * TO:3 * TO], in_=stt[0:1, 2 * TO:3 * TO], func=AF.Exp,
                scale=-0.5), waits=[t_ln], inc=True)
            t_nm = A("vector", lambda e: e.tensor_tensor(
                out=stt[0:1, 0:TO], in0=stt[0:1, 0:TO],
                in1=stt[0:1, 2 * TO:3 * TO], op=ALU.mult), waits=[t_rs], inc=True)
            if fp32_apply:
                t_ng = A("vector", lambda e: e.tensor_scalar_mul(
                    out=stt[0:1, 0:TO], in0=stt[0:1, 0:TO], scalar1=-1.0),
                    waits=[t_nm], inc=True)
                rsrc, bsrc = stt[0:1, 2 * TO:3 * TO], stt[0:1, 0:TO]
                lhs_one = onesr32[0:1, :]
                dst = abuf2
            else:
                A("vector", lambda e: e.tensor_copy(
                    out=stt16[0:1, 0:TO], in_=stt[0:1, 2 * TO:3 * TO]), waits=[t_rs])
                t_ng = A("vector", lambda e: e.tensor_scalar(
                    out=stt16[0:1, TO:2 * TO], in0=stt[0:1, 0:TO],
                    scalar1=-1.0, scalar2=None, op0=ALU.mult), waits=[t_nm], inc=True)
                rsrc, bsrc = stt16[0:1, 0:TO], stt16[0:1, TO:2 * TO]
                lhs_one = onesr16[0:1, :]
                dst = abuf
            toks = []
            for which, srcv in ((0, rsrc), (1, bsrc)):
                r, fr = rot_acquire()
                tm = A("tensor", lambda e, srcv=srcv, r=r, lhs_one=lhs_one: e.matmul(
                    psr[r][:, :], lhs_one, srcv, start=True, stop=True),
                    waits=[t_ng, fr, t_or16], inc=True)
                tv = A("vector", lambda e, which=which, r=r, dst=dst: e.tensor_copy(
                    out=dst[:, which * TO:(which + 1) * TO], in_=psr[r][:, :]),
                    waits=[tm], inc=True)
                rot_free[r] = tv
                toks.append(tv)
                tk_ln_free[0] = tm
            if dst_fn is None:
                return toks, dst
            t_apply = [None] * NI
            for o in range(NI):
                war = [apply_war_fn(o)] if apply_war_fn else []
                t_a1 = A("vector", lambda e, o=o, dst=dst: e.tensor_tensor(
                    out=dst_fn(o), in0=src_slice_fn(o),
                    in1=dst[:, 0:TO], op=ALU.mult),
                    waits=[rd_tok[o]] + toks + war, inc=True)
                t_apply[o] = A("vector", lambda e, o=o, dst=dst: e.tensor_tensor(
                    out=dst_fn(o), in0=dst_fn(o),
                    in1=dst[:, TO:2 * TO], op=ALU.add),
                    waits=[t_a1], inc=True)
            return t_apply

        def back0(c):
            pc = c % 2
            xo = pc * NI
            s = st[c]
            t_ymix = s["t_ymix"]
            t_r1 = [None] * NI
            t_op = None
            for o in range(NI):
                sl = o % 2
                tdw = A("sync", lambda e, o=o, sl=sl: e.dma_start(
                    out=wobuf[:, sl * NJ * P:(sl + 1) * NJ * P], in_=wo_ld.ap()[o]),
                    waits=[tk_wo_slot[sl]], inc=True, sem="wo0" if sl == 0 else "wo1")
                r, fr = rot_acquire()

                def op_mms(e, o=o, sl=sl, r=r):
                    last = None
                    for j in range(NJ):
                        last = e.matmul(psr[r][:, :],
                                        wobuf[:, (sl * NJ + j) * P:(sl * NJ + j + 1) * P],
                                        ymix[:, j, :], start=(j == 0), stop=(j == NJ - 1))
                    return last
                t_op = A("tensor", op_mms, waits=[tdw, fr, t_ymix[NJ - 1]], inc=True)
                tk_wo_slot[sl] = t_op
                t_r1[o] = A("vector", lambda e, o=o, r=r, xo=xo: e.tensor_tensor(
                    out=x16[:, xo + o, 3:3 + TO], in0=x16[:, xo + o, 3:3 + TO],
                    in1=psr[r][:, :], op=ALU.add), waits=[t_op], inc=True)
                rot_free[r] = t_r1[o]
            tk_ymix_free[:] = [t_op]
            s["t_r1"] = t_r1

        def back1(c):
            pc = c % 2
            xo = pc * NI
            s = st[c]
            t_r1 = s["t_r1"]
            t_x1 = layer_norm(t_r1, False,
                              dst_fn=lambda o: x1b[:, o, :],
                              src_slice_fn=lambda o, xo=xo: x16[:, xo + o, 3:3 + TO],
                              apply_war_fn=lambda o: tk_x1_free[0])
            s["t_x1"] = t_x1

        def back2(c):
            pc = c % 2
            xo = pc * NI
            s = st[c]
            t_x1 = s["t_x1"]
            t_r2 = [None] * NI
            t_ff = None
            for o in range(NI):
                sl = o % 2
                tdwf = A("sync", lambda e, o=o, sl=sl: e.dma_start(
                    out=wfbuf[:, sl * NI * P:(sl + 1) * NI * P], in_=wf_ld.ap()[o]),
                    waits=[tk_wf_slot[sl]], inc=True, sem="wf0" if sl == 0 else "wf1")
                r, fr = rot_acquire()

                def ff_mms(e, o=o, sl=sl, r=r):
                    last = None
                    for i in range(NI):
                        last = e.matmul(psr[r][:, :],
                                        wfbuf[:, (sl * NI + i) * P:(sl * NI + i + 1) * P],
                                        x1b[:, i, :], start=(i == 0), stop=(i == NI - 1))
                    return last
                t_ff = A("tensor", ff_mms, waits=[tdwf, fr, t_x1[NI - 1]], inc=True)
                tk_wf_slot[sl] = t_ff
                t_r2[o] = A("vector", lambda e, o=o, r=r, xo=xo: e.tensor_tensor(
                    out=x16[:, xo + o, 3:3 + TO], in0=x1b[:, o, :], in1=psr[r][:, :],
                    op=ALU.add), waits=[t_ff], inc=True)
                rot_free[r] = t_r2[o]
            tk_x1_free[:] = [t_ff]
            s["t_r2"] = t_r2

        def back3(c):
            pc = c % 2
            xo = pc * NI
            s = st[c]
            t_r2 = s["t_r2"]
            toks2, dst2 = layer_norm(t_r2, True, dst_fn=None,
                                     src_slice_fn=lambda o, xo=xo: x16[:, xo + o, 3:3 + TO])
            t_tp_last_per_tt = [None] * 4
            t_o2_last = None
            t0 = c * TO
            for o in range(NI):
                sl = o % 2
                t_oa = A("vector", lambda e, o=o, sl=sl, xo=xo: e.tensor_tensor(
                    out=outf[:, sl * TO:(sl + 1) * TO],
                    in0=x16[:, xo + o, 3:3 + TO], in1=dst2[:, 0:TO],
                    op=ALU.mult),
                    waits=[t_r2[o]] + toks2
                    + ([tk_outf_slot[sl]] if tk_outf_slot[sl] else []), inc=True)
                t_o2_last = A("vector", lambda e, sl=sl: e.tensor_tensor(
                    out=outf[:, sl * TO:(sl + 1) * TO],
                    in0=outf[:, sl * TO:(sl + 1) * TO],
                    in1=dst2[:, TO:2 * TO], op=ALU.add),
                    waits=[t_oa], inc=True)
                for tt in range(4):
                    r, fr = rot_acquire()
                    tm = A("tensor", lambda e, tt=tt, r=r, sl=sl: e.transpose(
                        psr[r][:, :P], outf[:, sl * TO + tt * P:sl * TO + (tt + 1) * P],
                        ident[:, :]), waits=[t_o2_last, fr], inc=True)
                    ta = A("scalar", lambda e, o=o, tt=tt, r=r: e.activation(
                        out=otok[:, tt * DM + o * P:tt * DM + (o + 1) * P],
                        in_=psr[r][:, :P], func=AF.Copy),
                        waits=[tm] + ([tk_ot_slot[tt]] if tk_ot_slot[tt] else []), inc=True)
                    rot_free[r] = ta
                    t_tp_last_per_tt[tt] = ta
                    tk_outf_slot[sl] = tm
            for tt in range(4):
                tdo = A("gpsimd", lambda e, t0=t0, tt=tt: e.dma_start(
                    out=yout[t0 + tt * P:t0 + (tt + 1) * P, :],
                    in_=otok[:, tt * DM:(tt + 1) * DM]),
                    waits=[t_tp_last_per_tt[tt]], inc=True, inck=16,
                    sem="ot%d" % tt)
                tk_ot_slot[tt] = tdo
            tk_x16_pp[pc] = [t_o2_last, tk_sq_slot[0], tk_sq_slot[1]]

        front_a(0)
        for j in range(NJ):
            front_b(0, j)
        front_tail(0)
        for c in range(NCH):
            if c > 0:
                back0(c - 1)
            hooks = {}
            if c > 0:
                hooks[1] = lambda cc=c - 1: back1(cc)
                hooks[2] = lambda cc=c - 1: back2(cc)
                hooks[3] = lambda cc=c - 1: back3(cc)
            if c + 1 < NCH:
                hooks[4] = lambda cc=c + 1: front_a(cc)
                hooks[5] = lambda cc=c + 1: [front_b(cc, _j) for _j in range(6)]
                hooks[6] = lambda cc=c + 1: [front_b(cc, _j) for _j in range(6, 12)]
            scan(c, hooks)
        back0(NCH - 1)
        back1(NCH - 1)
        back2(NCH - 1)
        back3(NCH - 1)

        finals = [(s, pr.counts[s]) for s in ("sync",) + Prog.CHANNELS
                  if pr.counts[s] > 0]

        def final_waits(e):
            for s, v in finals:
                e.wait_ge(pr.sems[s], v)
        A("sync", final_waits)

        block = ent(nc.Block())
        pr.emit(block)
    return nc


_NC_CACHE = [None]


def kernel(L0_attn_x, L0_attn_W_in, L0_attn_conv_k, L0_attn_conv_b,
           L0_attn_W_xproj, L0_attn_W_dt, L0_attn_b_dt, L0_attn_A_log,
           L0_attn_D, L0_attn_W_out, L0_ffn_weight):
    f32 = np.float32
    x = np.asarray(L0_attn_x, f32)
    W_in = np.ascontiguousarray(np.asarray(L0_attn_W_in, f32))
    ck = np.ascontiguousarray(np.asarray(L0_attn_conv_k, f32)[:, 0, :])
    cb = np.ascontiguousarray(np.asarray(L0_attn_conv_b, f32).reshape(NJ, P).T)
    wxp = np.ascontiguousarray(
        np.asarray(L0_attn_W_xproj, f32).reshape(NJ, P, 160).transpose(1, 0, 2).reshape(P, NJ * 160))
    wdt = np.ascontiguousarray(np.asarray(L0_attn_W_dt, f32))
    bdt = np.ascontiguousarray(np.asarray(L0_attn_b_dt, f32).reshape(NJ, P).T)
    ap_ = np.ascontiguousarray(
        np.asarray(L0_attn_A_log, f32).reshape(NJ, P, NST).transpose(1, 0, 2).reshape(P, NJ * NST))
    dp = np.ascontiguousarray(np.asarray(L0_attn_D, f32).reshape(NJ, P).T)
    wout = np.ascontiguousarray(np.asarray(L0_attn_W_out, f32))
    wffn = np.ascontiguousarray(np.asarray(L0_ffn_weight, f32))
    ident = np.eye(P, dtype=f32)
    onescol = np.zeros((P, 2), f32)
    onescol[:, 0] = 1.0
    onesrow = np.ones((1, P), f32)
    selc = np.zeros((32, 2 * DI), f32)
    for n in range(NST):
        selc[n, n * P:(n + 1) * P] = -1.0
        selc[NST + n, DI + n * P:DI + (n + 1) * P] = 1.0

    if _NC_CACHE[0] is None:
        _NC_CACHE[0] = build_nc()
    nc = _NC_CACHE[0]

    shared = dict(w_in=W_in, conv_k=ck, cb_perm=cb, wxp_perm=wxp, w_dt=wdt,
                  bdt_perm=bdt, a_perm=ap_, d_perm=dp, w_out=wout, w_ffn=wffn,
                  identc=ident, onescol=onescol, onesrow=onesrow, selc=selc)
    in_maps = [dict(shared, xin=np.ascontiguousarray(x[b])) for b in range(8)]
    res = run_bass_kernel_spmd(nc, in_maps, core_ids=list(range(8)))
    out = np.stack([np.asarray(res.results[b]["yout"]) for b in range(8)], axis=0)
    return out


# version 26 (final)
# speedup vs baseline: 1.0178x; 1.0017x over previous
import contextlib
import numpy as np

import concourse.bass as bass
import concourse.mybir as mybir
from concourse.bass_utils import run_bass_kernel_spmd

F32 = mybir.dt.float32
BF16 = mybir.dt.bfloat16
AF = mybir.ActivationFunctionType
ALU = mybir.AluOpType

P = 128
L = 4096
DM = 1024
DI = 2048
NST = 16
TO = 512
NCH = L // TO
NJ = DI // P
NI = DM // P


class Prog:

    ENGS = ("tensor", "vector", "scalar", "sync", "gpsimd")
    CHANNELS = ("wt0", "wt1", "w2a", "w2b", "wo0", "wo1", "wf0", "wf1",
                "xs", "ot0", "ot1", "ot2", "ot3",
                "gt0", "gt1", "gt2", "gt3")

    def __init__(self, nc, ctx):
        self.nc = nc
        self.ops = {e: [] for e in self.ENGS}
        self.sems = {e: ctx.enter_context(nc.semaphore("sem_" + e)) for e in self.ENGS}
        for ch in self.CHANNELS:
            self.sems[ch] = ctx.enter_context(nc.semaphore("sem_" + ch))
        self.counts = {s: 0 for s in self.sems}

    def add(self, eng, fn, waits=(), inc=False, inck=None, sem=None):
        tok = None
        s = sem if sem is not None else eng
        k = inck if inck is not None else (16 if eng == "sync" else 1)
        if inc:
            self.counts[s] += k
            tok = (s, self.counts[s])
        ws = []
        for w in waits:
            if w is None:
                continue
            if isinstance(w, list):
                ws.extend(x for x in w if x is not None)
            else:
                ws.append(w)
        self.ops[eng].append((ws, fn, (s, k) if inc else None))
        return tok

    def emit(self, block):
        for eng in self.ENGS:
            ops = self.ops[eng]
            if not ops:
                continue
            sems = self.sems

            def make(ops, eng):
                def body(e):
                    seen = {}
                    for ws, fn, incsk in ops:
                        for (psem, v) in ws:
                            if seen.get(psem, 0) >= v:
                                continue
                            seen[psem] = v
                            e.wait_ge(sems[psem], v)
                        last = fn(e)
                        if incsk is not None:
                            s, k = incsk
                            if last is None:
                                last = e.engine_nop() if eng in ("vector", "gpsimd") else e.nop()
                            last.then_inc(sems[s], k)
                return body
            getattr(block, eng)(make(ops, eng))


def build_nc():
    nc = bass.Bass("TRN2", target_bir_lowering=False)

    xin = nc.dram_tensor("xin", [L, DM], F32, kind="ExternalInput").ap()
    w_in = nc.dram_tensor("w_in", [DM, 2 * DI], F32, kind="ExternalInput").ap()
    conv_k = nc.dram_tensor("conv_k", [4, DI], F32, kind="ExternalInput").ap()
    cb_perm = nc.dram_tensor("cb_perm", [P, NJ], F32, kind="ExternalInput").ap()
    wxp_perm = nc.dram_tensor("wxp_perm", [P, NJ * 160], F32, kind="ExternalInput").ap()
    w_dt = nc.dram_tensor("w_dt", [P, DI], F32, kind="ExternalInput").ap()
    bdt_perm = nc.dram_tensor("bdt_perm", [P, NJ], F32, kind="ExternalInput").ap()
    a_perm = nc.dram_tensor("a_perm", [P, NJ * NST], F32, kind="ExternalInput").ap()
    d_perm = nc.dram_tensor("d_perm", [P, NJ], F32, kind="ExternalInput").ap()
    w_out = nc.dram_tensor("w_out", [DI, DM], F32, kind="ExternalInput").ap()
    w_ffn = nc.dram_tensor("w_ffn", [DM, DM], F32, kind="ExternalInput").ap()
    identc = nc.dram_tensor("identc", [P, P], F32, kind="ExternalInput").ap()
    onescol = nc.dram_tensor("onescol", [P, 2], F32, kind="ExternalInput").ap()
    onesrow = nc.dram_tensor("onesrow", [1, P], F32, kind="ExternalInput").ap()
    selc = nc.dram_tensor("selc", [32, 2 * DI], F32, kind="ExternalInput").ap()
    yout = nc.dram_tensor("yout", [L, DM], F32, kind="ExternalOutput").ap()

    wt_ld = nc.dram_tensor("wt_ld", [NJ, P, 32 * P], BF16)
    w2_ld = nc.dram_tensor("w2_ld", [NJ, P, NI * P], BF16)
    wo_ld = nc.dram_tensor("wo_ld", [NI, P, NJ * P], BF16)
    wf_ld = nc.dram_tensor("wf_ld", [NI, P, NI * P], BF16)

    with contextlib.ExitStack() as ctx:
        ent = ctx.enter_context
        pr = Prog(nc, ctx)
        A = pr.add
        p0ch = {}

        def SD(fn, waits=(), ch="sync"):
            t = pr.add("sync", fn, waits=list(waits) + [p0ch.get(ch)],
                       inc=True, sem=ch if ch != "sync" else None)
            p0ch[ch] = t
            return t

        psr = [ent(nc.psum_tensor(f"psr{i}", [P, TO], F32)) for i in range(4)]
        psx0 = ent(nc.psum_tensor("psx0", [P, TO], F32))
        psx1 = ent(nc.psum_tensor("psx1", [32, TO], F32))
        pss0 = ent(nc.psum_tensor("pss0", [1, TO], F32))
        pss1 = ent(nc.psum_tensor("pss1", [1, TO], F32))
        rot_free = [None] * 4
        rot_i = [0]

        def rot_acquire():
            r = rot_i[0] % 4
            rot_i[0] += 1
            return r, rot_free[r]

        ident = ent(nc.sbuf_tensor("ident", [P, P], F32))
        ones32 = ent(nc.sbuf_tensor("ones32", [P, 2], F32))
        ones16 = ent(nc.sbuf_tensor("ones16", [P, 2], BF16))
        onesr32 = ent(nc.sbuf_tensor("onesr32", [1, P], F32))
        onesr16 = ent(nc.sbuf_tensor("onesr16", [1, P], BF16))
        sel16 = ent(nc.sbuf_tensor("sel16", [32, 2 * DI], BF16))
        apos = ent(nc.sbuf_tensor("apos", [P, NJ * NST], F32))
        cbias = ent(nc.sbuf_tensor("cbias", [P, NJ], F32))
        nbdt = ent(nc.sbuf_tensor("nbdt", [P, NJ], F32))
        dvec = ent(nc.sbuf_tensor("dvec", [P, NJ], F32))
        wxp16 = ent(nc.sbuf_tensor("wxp16", [P, NJ * 160], BF16))
        wdt16 = ent(nc.sbuf_tensor("wdt16", [P, DI], BF16))

        t_id = SD(lambda e: e.dma_start(out=ident[:, :], in_=identc), ch="ot0")
        t_oc = SD(lambda e: e.dma_start(out=ones32[:, :], in_=onescol), ch="ot1")
        t_or = SD(lambda e: e.dma_start(out=onesr32[:, :], in_=onesrow), ch="ot2")
        t_o16 = A("vector", lambda e: e.tensor_copy(out=ones16[:, :], in_=ones32[:, :]), waits=[t_oc], inc=True)
        t_or16 = A("vector", lambda e: e.tensor_copy(out=onesr16[:, :], in_=onesr32[:, :]),
                   waits=[t_or], inc=True)
        t = SD(lambda e: e.dma_start(out=apos[:, :], in_=a_perm), ch="ot3")
        t_ap = A("scalar", lambda e: e.activation(out=apos[:, :], in_=apos[:, :], func=AF.Exp), waits=[t], inc=True)
        t_cb = SD(lambda e: e.dma_start(out=cbias[:, :], in_=cb_perm), ch="xs")
        t = SD(lambda e: e.dma_start(out=nbdt[:, :], in_=bdt_perm), ch="ot0")
        t_nb = A("vector", lambda e: e.tensor_scalar_mul(out=nbdt[:, :], in0=nbdt[:, :],
                                                         scalar1=-1.0), waits=[t], inc=True)
        t_dv = SD(lambda e: e.dma_start(out=dvec[:, :], in_=d_perm), ch="ot1")

        with nc.sbuf_tensor("p0sel", [32, 2 * DI], F32) as p0sel, \
             nc.sbuf_tensor("p0big", [P, 4096], F32) as p0big, \
             nc.sbuf_tensor("p0kb", [P, 4 * DI], F32) as p0kb, \
             nc.sbuf_tensor("p0ck", [1, DI], F32) as p0ck, \
             nc.sbuf_tensor("p0wst", [P, 16 * P], F32) as p0wst, \
             nc.sbuf_tensor("p0wj", [P, 4 * 32 * P], BF16) as p0wj, \
             nc.sbuf_tensor("p0w2", [P, 4 * NI * P], BF16) as p0w2, \
             nc.sbuf_tensor("p0wo", [P, DM], BF16) as p0wo:

            t = SD(lambda e: e.dma_start(out=p0sel[:, :], in_=selc), ch="ot2")
            t_sel = A("vector", lambda e: e.tensor_copy(out=sel16[:, :], in_=p0sel[:, :]), waits=[t], inc=True)
            t = SD(lambda e: e.dma_start(out=p0big[:, :NJ * 160], in_=wxp_perm), ch="ot3")
            t_wxp = A("vector", lambda e: e.tensor_copy(out=wxp16[:, :], in_=p0big[:, :NJ * 160]), waits=[t], inc=True)
            t = SD(lambda e: e.dma_start(out=p0big[:, :DI], in_=w_dt),
                  waits=[t_wxp], ch="ot3")
            t_wdt = A("vector", lambda e: e.tensor_copy(out=wdt16[:, :], in_=p0big[:, :DI]), waits=[t], inc=True)

            tprev_ck = t_wdt
            tv = None
            for k in range(4):
                tck = SD(lambda e, k=k: e.dma_start(out=p0ck[:, :], in_=conv_k[k:k + 1, :]),
                        waits=[tprev_ck])
                for q in range(4):
                    r, fr = rot_acquire()
                    tm = A("tensor", lambda e, k=k, q=q, r=r: e.matmul(
                        psr[r][:, :], onesr32[0:1, :], p0ck[0:1, q * TO:(q + 1) * TO],
                        start=True, stop=True), waits=[tck, fr, t_or], inc=True)
                    tv = A("vector", lambda e, k=k, q=q, r=r: e.tensor_copy(
                        out=p0kb[:, k * DI + q * TO:k * DI + (q + 1) * TO], in_=psr[r][:, :]),
                        waits=[tm], inc=True)
                    rot_free[r] = tv
                tprev_ck = tv
            t_kb = tv

            wt_wr = [None] * NJ
            w2_wr = [None] * NJ
            t_wj_dma = None
            t_w2_dma = None
            last_tt = None
            last_c2 = None
            Q = 4
            for j4 in range(0, NJ, Q):
                ltt = [None] * Q
                lc2 = [None] * Q
                for i in range(NI):
                    sl = i % 2
                    td = SD(lambda e, j4=j4, i=i, sl=sl: e.dma_start(
                        out=p0wst[:, sl * Q * P:(sl + 1) * Q * P],
                        in_=w_in[i * P:(i + 1) * P, j4 * P:(j4 + Q) * P]),
                        waits=[last_tt if i >= 2 else None, t_wj_dma],
                        ch="wt1" if sl == 0 else "w2b")
                    for jq in range(Q):
                        j = j4 + jq
                        for k in range(4):
                            last_tt = A("vector", lambda e, jq=jq, j=j, i=i, k=k, sl=sl: e.tensor_tensor(
                                out=p0wj[:, (jq * 32 + i * 4 + k) * P:(jq * 32 + i * 4 + k + 1) * P],
                                in0=p0wst[:, (sl * Q + jq) * P:(sl * Q + jq + 1) * P],
                                in1=p0kb[:, k * DI + j * P:k * DI + (j + 1) * P], op=ALU.mult),
                                waits=[td, t_kb], inc=True)
                        ltt[jq] = last_tt
                    sl2 = i % 2 + 2
                    td2 = SD(lambda e, j4=j4, i=i, sl2=sl2: e.dma_start(
                        out=p0wst[:, sl2 * Q * P:(sl2 + 1) * Q * P],
                        in_=w_in[i * P:(i + 1) * P, DI + j4 * P:DI + (j4 + Q) * P]),
                        waits=[last_c2 if i >= 2 else None, t_w2_dma],
                        ch="wo1" if sl2 == 2 else "wf1")
                    for jq in range(Q):
                        last_c2 = A("vector", lambda e, jq=jq, i=i, sl2=sl2: e.tensor_copy(
                            out=p0w2[:, (jq * NI + i) * P:(jq * NI + i + 1) * P],
                            in_=p0wst[:, (sl2 * Q + jq) * P:(sl2 * Q + jq + 1) * P]),
                            waits=[td2], inc=True)
                        lc2[jq] = last_c2
                for jq in range(Q):
                    j = j4 + jq
                    t_wj_dma = SD(lambda e, j=j, jq=jq: e.dma_start(
                        out=wt_ld.ap()[j], in_=p0wj[:, jq * 32 * P:(jq + 1) * 32 * P]),
                        waits=[ltt[jq]], ch="wt0")
                    t_w2_dma = SD(lambda e, j=j, jq=jq: e.dma_start(
                        out=w2_ld.ap()[j], in_=p0w2[:, jq * NI * P:(jq + 1) * NI * P]),
                        waits=[lc2[jq]], ch="w2a")
                    wt_wr[j] = t_wj_dma
                    w2_wr[j] = t_w2_dma

            t_wo_dma = None
            tcast = None
            for j in range(NJ):
                td = SD(lambda e, j=j: e.dma_start(
                    out=p0big[:, :DM], in_=w_out[j * P:(j + 1) * P, :]),
                    waits=[tcast, t_wxp, t_wo_dma,
                           t_wdt if j == 0 else None],
                    ch="ot0" if j % 2 == 0 else "ot1")
                tcast = A("vector", lambda e: e.tensor_copy(out=p0wo[:, :], in_=p0big[:, :DM]),
                          waits=[td, t_wo_dma], inc=True)
                for o in range(NI):
                    t_wo_dma = SD(lambda e, j=j, o=o: e.dma_start(
                        out=wo_ld.ap()[o][:, j * P:(j + 1) * P],
                        in_=p0wo[:, o * P:(o + 1) * P]), waits=[tcast], ch="wo0")
            wo_wr = t_wo_dma

            t_sgn = None
            t_wf_dma = None
            for ot in range(NI):
                td = SD(lambda e, ot=ot: e.dma_start(
                    out=p0big[:, :DM], in_=w_ffn[ot * P:(ot + 1) * P, :]),
                    waits=[t_sgn, tcast if ot == 0 else None,
                           t_wo_dma if ot == 0 else None],
                    ch="ot2" if ot % 2 == 0 else "ot3")
                for ib in range(NI):
                    r, fr = rot_acquire()
                    tm = A("tensor", lambda e, ib=ib, r=r: e.transpose(
                        psr[r][:, :P], p0big[:, ib * P:(ib + 1) * P], ident[:, :]),
                        waits=[td, fr, t_id], inc=True)
                    t_sgn = A("scalar", lambda e, ib=ib, r=r: e.activation(
                        out=p0wo[:, ib * P:(ib + 1) * P],
                        in_=psr[r][:, :P], func=AF.Sign),
                        waits=[tm, t_wf_dma], inc=True)
                    rot_free[r] = t_sgn
                t_wf_dma = SD(lambda e, ot=ot: e.dma_start(
                    out=wf_ld.ap()[ot], in_=p0wo[:, :]), waits=[t_sgn], ch="wf0")
            wf_wr = t_wf_dma

        wtbuf = ent(nc.sbuf_tensor("wtbuf", [P, 2 * 32 * P], BF16))
        w2buf = ent(nc.sbuf_tensor("w2buf", [P, 2 * NI * P], BF16))
        wobuf = ent(nc.sbuf_tensor("wobuf", [P, 2 * NJ * P], BF16))
        wfbuf = ent(nc.sbuf_tensor("wfbuf", [P, 2 * NI * P], BF16))
        xstag = ent(nc.sbuf_tensor("xstag", [P, DM], F32))
        x16 = ent(nc.sbuf_tensor("x16", [P, 2 * NI, TO + 3], BF16))
        u16 = ent(nc.sbuf_tensor("u16", [P, NJ, TO], BF16))
        dtl16 = ent(nc.sbuf_tensor("dtl16", [P, TO], BF16))
        bc16 = ent(nc.sbuf_tensor("bc16", [32, TO], BF16))
        bneg = ent(nc.sbuf_tensor("bneg", [P, NST, TO], BF16))
        cpos = ent(nc.sbuf_tensor("cpos", [P, NST, TO], BF16))
        dtneg = ent(nc.sbuf_tensor("dtneg", [P, 2 * TO], F32))
        szb = ent(nc.sbuf_tensor("szb", [P, 2 * TO], BF16))
        w16b = ent(nc.sbuf_tensor("w16b", [P, 2 * TO], BF16))
        dab = ent(nc.sbuf_tensor("dab", [P, 8 * TO], BF16))
        dbb = ent(nc.sbuf_tensor("dbb", [P, 2 * TO], BF16))
        hb = ent(nc.sbuf_tensor("hb", [P, 4 * TO], BF16))
        yjb = ent(nc.sbuf_tensor("yjb", [P, 2 * TO], BF16))
        ymix = ent(nc.sbuf_tensor("ymix", [P, NJ, TO], BF16))
        sqtmp = ent(nc.sbuf_tensor("sqtmp", [P, 2 * TO], BF16))
        x1b = ent(nc.sbuf_tensor("x1b", [P, NI, TO], BF16))
        outf = ent(nc.sbuf_tensor("outf", [P, 2 * TO], F32))
        otok = ent(nc.sbuf_tensor("otok", [P, 4 * DM], F32))
        state = ent(nc.sbuf_tensor("state", [P, NJ * NST], F32))
        stt = ent(nc.sbuf_tensor("stt", [1, 3 * TO], F32))
        epsc = ent(nc.sbuf_tensor("epsc", [1, 2], F32))
        tinyb = ent(nc.sbuf_tensor("tinyb", [P, 2], F32))
        stt16 = ent(nc.sbuf_tensor("stt16", [1, 2 * TO], BF16))
        abuf = ent(nc.sbuf_tensor("abuf", [P, 2 * TO], BF16))
        abuf2 = ent(nc.sbuf_tensor("abuf2", [P, 2 * TO], F32))
        flsh = ent(nc.sbuf_tensor("flsh", [P, 8], F32))

        for _eng in ("vector", "scalar", "tensor", "gpsimd"):
            A(_eng, lambda e: None, waits=list(p0ch.values()), inc=True)
        t_ms = A("vector", lambda e: e.memset(state[:, :], 0.0), inc=True)
        t_eps = A("vector", lambda e: e.memset(epsc[0:1, :], 1e-5), inc=True)
        t_tb = A("vector", lambda e: e.memset(tinyb[:, :], 1e-30), inc=True)
        t_mx = A("vector", lambda e: e.memset(x16[:, :, 0:3], 0.0), inc=True)

        tk_x16_pp = {0: [t_mx], 1: [t_mx]}
        tk_xz_hist = {0: None, 1: None}
        tk_tail = {0: t_mx, 1: t_mx}
        tk_u16_j = [None] * NJ
        tk_uxp = [None]
        tk_dtl_free = [None]
        tk_bc_free = [None]
        tk_bcpos_free = [None]
        tk_ymix_free = [None]
        tk_x1_free = [None]
        tk_psx_free = [None]
        tk_pss_free = [None, None]
        tk_wt_slot = [wt_wr[NJ - 1], [wt_wr[NJ - 1], p0ch.get("wt1")]]
        tk_w2_slot = [w2_wr[NJ - 1], [w2_wr[NJ - 1], p0ch.get("w2b")]]
        tk_wo_slot = [wo_wr, [wo_wr, p0ch.get("wo1")]]
        tk_wf_slot = [wf_wr, [wf_wr, p0ch.get("wf1")]]
        tk_xs_slot = [p0ch.get("xs")]
        tk_ot_slot = [p0ch.get("ot0"), p0ch.get("ot1"), p0ch.get("ot2"), p0ch.get("ot3")]
        tk_sz_slot = [None] * 2
        tk_dtn_slot = [None] * 2
        tk_da_slot = [None] * 8
        tk_sq_slot = [None, None]
        tk_outf_slot = [None] * 2
        tk_ln_free = [None]
        tk_dbb = [None, None]
        tk_hb = [None] * 4
        tk_yjb = [None, None]
        tk_w16 = [None, None]
        tk_dtn_act = [None, None]
        tk_state_last = [None]

        st = {}
        BARRIER = [False]

        def hard_sync():
            if not BARRIER[0]:
                return
            toks = [(e, pr.counts[e]) for e in ("tensor", "vector", "scalar")
                    if pr.counts[e] > 0]
            for e in ("tensor", "vector", "scalar"):
                A(e, lambda e2: None, waits=toks, inc=True)

        def front_a(c):
            t0 = c * TO
            pc = c % 2
            xo = pc * NI
            s = st[c] = {}
            ta_last = None
            for tt in range(4):
                td = A("sync", lambda e, t0=t0, tt=tt: e.dma_start(
                    out=xstag[:, :], in_=xin[t0 + tt * P:t0 + (tt + 1) * P, :]),
                    waits=[tk_xs_slot[0]], inc=True, sem="xs")
                tm = None
                for ib in range(NI):
                    r, fr = rot_acquire()
                    tm = A("tensor", lambda e, ib=ib, r=r: e.transpose(
                        psr[r][:, :P], xstag[:, ib * P:(ib + 1) * P], ident[:, :]),
                        waits=[td, fr] + tk_x16_pp[pc], inc=True)
                    ta_last = A("scalar", lambda e, tt=tt, ib=ib, r=r, xo=xo: e.activation(
                        out=x16[:, xo + ib, 3 + tt * P:3 + (tt + 1) * P], in_=psr[r][:, :P],
                        func=AF.Copy), waits=[tm] + tk_x16_pp[pc], inc=True)
                    rot_free[r] = ta_last
                tk_xs_slot[0] = tm
            s["ta_last"] = ta_last
            s["t_uev"] = [None] * NJ
            s["t_xp"] = None
            s["t_xz_last"] = None

        def front_b(c, j):
            pc = c % 2
            xo = pc * NI
            s = st[c]
            ta_last = s["ta_last"]
            sl = j % 2
            tdw = A("sync", lambda e, j=j, sl=sl: e.dma_start(
                out=wtbuf[:, sl * 32 * P:(sl + 1) * 32 * P], in_=wt_ld.ap()[j]),
                waits=[tk_wt_slot[sl]], inc=True, sem="wt0" if sl == 0 else "wt1")
            r, fr = rot_acquire()

            def xz_mms(e, j=j, sl=sl, r=r, xo=xo):
                last = None
                for i in range(NI):
                    for k in range(4):
                        last = e.matmul(
                            psr[r][:, :],
                            wtbuf[:, (sl * 32 + i * 4 + k) * P:(sl * 32 + i * 4 + k + 1) * P],
                            x16[:, xo + i, k:k + TO],
                            start=(i == 0 and k == 0), stop=(i == NI - 1 and k == 3))
                return last
            t_xz_last = A("tensor", xz_mms, waits=[tdw, fr, ta_last, tk_tail[pc]],
                          inc=True)
            tk_wt_slot[sl] = t_xz_last
            s["t_xz_last"] = t_xz_last
            prev_xp = st[c - 1]["t_xp"] if c > 0 else None
            s["t_uev"][j] = A("scalar", lambda e, j=j, r=r: e.activation(
                out=u16[:, j, :], in_=psr[r][:, :], func=AF.Silu,
                bias=cbias[:, j:j + 1], scale=1.0),
                waits=[t_xz_last, t_cb, tk_u16_j[j], prev_xp], inc=True)
            rot_free[r] = s["t_uev"][j]


        def front_tail(c):
            pc = c % 2
            xo = pc * NI
            s = st[c]
            ta_last = s["ta_last"]

            def xp_mms(e):
                last = None
                for j in range(NJ):
                    e.matmul(psx0[:, :], wxp16[:, j * 160:j * 160 + P], u16[:, j, :],
                             start=(j == 0), stop=(j == NJ - 1))
                    last = e.matmul(psx1[:, :], wxp16[:, j * 160 + P:j * 160 + 160],
                                    u16[:, j, :], start=(j == 0), stop=(j == NJ - 1))
                return last
            t_xp = A("tensor", xp_mms,
                     waits=[s["t_uev"][NJ - 1], t_wxp]
                     + [t for t in tk_psx_free if t], inc=True)
            s["t_xp"] = t_xp
            tk_tail[1 - pc] = A("vector", lambda e, xo=xo, pc=pc: e.tensor_copy(
                out=x16[:, (1 - pc) * NI:(1 - pc) * NI + NI, 0:3],
                in_=x16[:, xo:xo + NI, TO:TO + 3]),
                waits=[ta_last, tk_xz_hist[1 - pc]], inc=True)
            tk_xz_hist[pc] = s["t_xz_last"]
            t_dtl = A("vector", lambda e: e.tensor_copy(out=dtl16[:, :], in_=psx0[:, :]),
                      waits=[t_xp] + tk_dtl_free, inc=True)
            t_bc = A("vector", lambda e: e.tensor_copy(out=bc16[:, :], in_=psx1[:, :]),
                     waits=[t_xp] + tk_bc_free, inc=True)
            tk_psx_free[:] = [t_dtl, t_bc]
            s["t_dtl"] = t_dtl
            t_bn = [None] * NST
            t_cp = [None] * NST
            t_bcast_pe = None
            for n in range(NST):
                r, fr = rot_acquire()
                t_bcast_pe = A("tensor", lambda e, n=n, r=r: e.matmul(
                    psr[r][:, :], sel16[:, n * P:(n + 1) * P], bc16[:, :],
                    start=True, stop=True), waits=[t_bc, t_sel, fr], inc=True)
                t_bn[n] = A("vector", lambda e, n=n, r=r: e.tensor_copy(
                    out=bneg[:, n, :], in_=psr[r][:, :]),
                    waits=[t_bcast_pe] + tk_bcpos_free, inc=True)
                rot_free[r] = t_bn[n]
                r, fr = rot_acquire()
                t_bcast_pe = A("tensor", lambda e, n=n, r=r: e.matmul(
                    psr[r][:, :], sel16[:, DI + n * P:DI + (n + 1) * P], bc16[:, :],
                    start=True, stop=True), waits=[t_bc, t_sel, fr], inc=True)
                t_cp[n] = A("vector", lambda e, n=n, r=r: e.tensor_copy(
                    out=cpos[:, n, :], in_=psr[r][:, :]),
                    waits=[t_bcast_pe] + tk_bcpos_free, inc=True)
                rot_free[r] = t_cp[n]
            tk_bc_free[:] = [t_bcast_pe]
            s["t_bn"], s["t_cp"] = t_bn, t_cp

        def scan(c, hooks=None):
            pc = c % 2
            xo = pc * NI
            s = st[c]
            t_uev, t_dtl = s["t_uev"], s["t_dtl"]
            t_bn, t_cp = s["t_bn"], s["t_cp"]
            ta_last = s["ta_last"]
            prev_state = tk_state_last[0]
            t_ymix = [None] * NJ
            t_dtp = None
            for j0 in range(0, NJ, 2):
                pair = (j0, j0 + 1)
                t_dtp_p, t_z_p, rr1, rr2 = {}, {}, {}, {}
                for jj in pair:
                    sl = jj % 2
                    r1, fr1 = rot_acquire()
                    t_dtp_p[jj] = A("tensor", lambda e, jj=jj, r1=r1: e.matmul(
                        psr[r1][:, :], wdt16[:, jj * P:(jj + 1) * P], dtl16[:, :],
                        start=True, stop=True), waits=[t_dtl, t_wdt, fr1], inc=True)
                    rr1[jj] = r1
                    tdw2 = A("sync", lambda e, jj=jj, sl=sl: e.dma_start(
                        out=w2buf[:, sl * NI * P:(sl + 1) * NI * P], in_=w2_ld.ap()[jj]),
                        waits=[tk_w2_slot[sl]], inc=True, sem="w2a" if sl == 0 else "w2b")
                    r2, fr2 = rot_acquire()

                    def z_mms(e, jj=jj, sl=sl, r2=r2, xo=xo):
                        last = None
                        for i in range(NI):
                            last = e.matmul(psr[r2][:, :],
                                            w2buf[:, (sl * NI + i) * P:(sl * NI + i + 1) * P],
                                            x16[:, xo + i, 3:3 + TO],
                                            start=(i == 0), stop=(i == NI - 1))
                        return last
                    t_z_p[jj] = A("tensor", z_mms, waits=[tdw2, fr2, ta_last], inc=True)
                    tk_w2_slot[sl] = t_z_p[jj]
                    rr2[jj] = r2
                t_sg_p = {}
                for jj in pair:
                    sl = jj % 2
                    t_sg_p[jj] = A("scalar", lambda e, jj=jj, sl=sl, r1=rr1[jj]: e.activation(
                        out=dtneg[:, sl * TO:(sl + 1) * TO], in_=psr[r1][:, :],
                        func=AF.Sigmoid, scale=-1.0, bias=nbdt[:, jj:jj + 1]),
                        waits=[t_dtp_p[jj], t_nb, tk_dtn_slot[sl], tk_dtn_act[sl]],
                        inc=True)
                    rot_free[rr1[jj]] = t_sg_p[jj]
                t_sz_p = {}
                for jj in pair:
                    sl = jj % 2
                    t_sz_p[jj] = A("scalar", lambda e, sl=sl, r2=rr2[jj]: e.activation(
                        out=szb[:, sl * TO:(sl + 1) * TO], in_=psr[r2][:, :], func=AF.Silu),
                        waits=[t_z_p[jj], tk_sz_slot[sl]], inc=True)
                    rot_free[rr2[jj]] = t_sz_p[jj]
                t_dtn_p = {}
                for jj in pair:
                    sl = jj % 2
                    t_dtn_p[jj] = A("scalar", lambda e, sl=sl: e.activation(
                        out=dtneg[:, sl * TO:(sl + 1) * TO],
                        in_=dtneg[:, sl * TO:(sl + 1) * TO], func=AF.Ln,
                        bias=tinyb[:, 0:1]), waits=[t_tb, t_sg_p[jj]], inc=True)
                for jj in pair:
                    sl = jj % 2
                    t_w16 = A("vector", lambda e, jj=jj, sl=sl: e.tensor_tensor(
                        out=w16b[:, sl * TO:(sl + 1) * TO],
                        in0=dtneg[:, sl * TO:(sl + 1) * TO],
                        in1=u16[:, jj, :], op=ALU.mult),
                        waits=[t_dtn_p[jj], t_uev[jj], tk_w16[sl]], inc=True)
                    tk_dtn_slot[sl] = t_w16
                    t_acc = A("vector", lambda e, jj=jj, sl=sl: e.tensor_scalar(
                        out=yjb[:, sl * TO:(sl + 1) * TO], in0=u16[:, jj, :],
                        scalar1=dvec[:, jj:jj + 1], scalar2=None, op0=ALU.mult),
                        waits=[t_dv, t_uev[jj], tk_yjb[sl]], inc=True)
                    tk_u16_j[jj] = t_acc

                    def emit_exp(n, jj=jj, sl=sl):
                        sd = n % 8
                        return A("scalar", lambda e, jj=jj, n=n, sl=sl, sd=sd: e.activation(
                            out=dab[:, sd * TO:(sd + 1) * TO],
                            in_=dtneg[:, sl * TO:(sl + 1) * TO], func=AF.Exp,
                            scale=apos[:, jj * NST + n:jj * NST + n + 1]),
                            waits=[t_dtn_p[jj], t_ap, tk_da_slot[sd]], inc=True)
                    t_da_n = [None] * (NST + 1)
                    t_da_n[0] = emit_exp(0)
                    t_db = None
                    for n in range(NST):
                        sn = n % 2
                        sd = n % 8
                        s4 = n % 4
                        if n + 1 < NST:
                            t_da_n[n + 1] = emit_exp(n + 1)
                        else:
                            t_da_n[NST] = t_da_n[NST - 1]
                        t_db = A("vector", lambda e, n=n, sl=sl, sn=sn: e.tensor_tensor(
                            out=dbb[:, sn * TO:(sn + 1) * TO],
                            in0=w16b[:, sl * TO:(sl + 1) * TO],
                            in1=bneg[:, n, :], op=ALU.mult),
                            waits=[t_bn[n], t_w16, tk_dbb[sn]], inc=True)
                        idx = jj * NST + n
                        t_scan = A("vector", lambda e, idx=idx, sn=sn, sd=sd, s4=s4: e.tensor_tensor_scan(
                            out=hb[:, s4 * TO:(s4 + 1) * TO],
                            data0=dab[:, sd * TO:(sd + 1) * TO],
                            data1=dbb[:, sn * TO:(sn + 1) * TO],
                            initial=state[:, idx:idx + 1], op0=ALU.mult, op1=ALU.add),
                            waits=[t_da_n[n + 1], t_db, t_ms, tk_hb[s4], prev_state],
                            inc=True)
                        tk_da_slot[sd] = t_scan
                        tk_state_last[0] = A("vector", lambda e, idx=idx, s4=s4: e.tensor_copy(
                            out=state[:, idx:idx + 1],
                            in_=hb[:, (s4 + 1) * TO - 1:(s4 + 1) * TO]),
                            waits=[t_scan], inc=True)
                        t_hc = A("vector", lambda e, n=n, sn=sn, s4=s4: e.tensor_tensor(
                            out=dbb[:, sn * TO:(sn + 1) * TO],
                            in0=hb[:, s4 * TO:(s4 + 1) * TO], in1=cpos[:, n, :],
                            op=ALU.mult), waits=[t_cp[n], t_scan], inc=True)
                        tk_hb[s4] = t_hc
                        t_acc = A("vector", lambda e, sl=sl, sn=sn: e.tensor_tensor(
                            out=yjb[:, sl * TO:(sl + 1) * TO],
                            in0=yjb[:, sl * TO:(sl + 1) * TO],
                            in1=dbb[:, sn * TO:(sn + 1) * TO], op=ALU.add),
                            waits=[t_hc, t_acc], inc=True)
                        tk_dbb[sn] = t_acc
                    tk_w16[sl] = t_db
                    tk_dtn_act[sl] = t_da_n[NST - 1]
                    t_ymix[jj] = A("vector", lambda e, jj=jj, sl=sl: e.tensor_tensor(
                        out=ymix[:, jj, :], in0=yjb[:, sl * TO:(sl + 1) * TO],
                        in1=szb[:, sl * TO:(sl + 1) * TO], op=ALU.mult),
                        waits=[t_sz_p[jj], t_acc] + (tk_ymix_free if jj <= 1 else []),
                        inc=True)
                    tk_sz_slot[sl] = t_ymix[jj]
                    tk_yjb[sl] = t_ymix[jj]
                t_dtp = t_dtp_p[pair[-1]]
                hk = (hooks or {}).get(j0 // 2)
                if hk:
                    hk()
            tk_dtl_free[:] = [t_dtp]
            tk_bcpos_free[:] = [t_ymix[NJ - 1]]
            s["t_ymix"] = t_ymix
            if c + 1 < NCH:
                for _j in range(12, NJ):
                    front_b(c + 1, _j)
                front_tail(c + 1)

        def layer_norm(rd_tok, fp32_apply, dst_fn, src_slice_fn, apply_war_fn=None):
            t_sq = [None] * (NI + 1)
            t_st = None

            def emit_sq(o):
                sl = o % 2
                return A("scalar", lambda e, o=o, sl=sl: e.activation(
                    out=sqtmp[:, sl * TO:(sl + 1) * TO], in_=src_slice_fn(o),
                    func=AF.Square),
                    waits=[rd_tok[o]]
                    + ([tk_sq_slot[sl]] if tk_sq_slot[sl] else []), inc=True)
            t_sq[0] = emit_sq(0)
            for o in range(NI):
                sl = o % 2
                if o + 1 < NI:
                    t_sq[o + 1] = emit_sq(o + 1)
                else:
                    t_sq[NI] = t_sq[NI - 1]

                def st_mms(e, o=o, sl=sl):
                    e.matmul(pss0[0:1, :], ones16[:, 0:1], src_slice_fn(o),
                             start=(o == 0), stop=(o == NI - 1))
                    return e.matmul(pss1[0:1, :], ones16[:, 0:1],
                                    sqtmp[:, sl * TO:(sl + 1) * TO],
                                    start=(o == 0), stop=(o == NI - 1))
                t_st = A("tensor", st_mms,
                         waits=[rd_tok[o], t_sq[o + 1], t_o16, t_st]
                         + ([t for t in tk_pss_free if t] if o == 0 else []), inc=True)
                tk_sq_slot[sl] = t_st
            t_mn = A("scalar", lambda e: e.activation(
                out=stt[0:1, 0:TO], in_=pss0[0:1, :], func=AF.Copy, scale=1.0 / DM),
                waits=[t_st] + tk_ln_free, inc=True)
            t_mq = A("scalar", lambda e: e.activation(
                out=stt[0:1, TO:2 * TO], in_=pss1[0:1, :], func=AF.Copy, scale=1.0 / DM),
                waits=[t_st], inc=True)
            tk_pss_free[0], tk_pss_free[1] = t_mn, t_mq
            t_mm = A("vector", lambda e: e.tensor_tensor(
                out=stt[0:1, 2 * TO:3 * TO], in0=stt[0:1, 0:TO], in1=stt[0:1, 0:TO],
                op=ALU.mult), waits=[t_mn], inc=True)
            t_var = A("vector", lambda e: e.tensor_tensor(
                out=stt[0:1, TO:2 * TO], in0=stt[0:1, TO:2 * TO],
                in1=stt[0:1, 2 * TO:3 * TO], op=ALU.subtract),
                waits=[t_mq, t_mm], inc=True)
            t_ln = A("scalar", lambda e: e.activation(
                out=stt[0:1, 2 * TO:3 * TO], in_=stt[0:1, TO:2 * TO], func=AF.Ln,
                bias=epsc[0:1, 0:1]), waits=[t_var, t_eps], inc=True)
            t_rs = A("scalar", lambda e: e.activation(
                out=stt[0:1, 2 * TO:3 * TO], in_=stt[0:1, 2 * TO:3 * TO], func=AF.Exp,
                scale=-0.5), waits=[t_ln], inc=True)
            t_nm = A("vector", lambda e: e.tensor_tensor(
                out=stt[0:1, 0:TO], in0=stt[0:1, 0:TO],
                in1=stt[0:1, 2 * TO:3 * TO], op=ALU.mult), waits=[t_rs], inc=True)
            if fp32_apply:
                t_ng = A("vector", lambda e: e.tensor_scalar_mul(
                    out=stt[0:1, 0:TO], in0=stt[0:1, 0:TO], scalar1=-1.0),
                    waits=[t_nm], inc=True)
                rsrc, bsrc = stt[0:1, 2 * TO:3 * TO], stt[0:1, 0:TO]
                lhs_one = onesr32[0:1, :]
                dst = abuf2
            else:
                A("vector", lambda e: e.tensor_copy(
                    out=stt16[0:1, 0:TO], in_=stt[0:1, 2 * TO:3 * TO]), waits=[t_rs])
                t_ng = A("vector", lambda e: e.tensor_scalar(
                    out=stt16[0:1, TO:2 * TO], in0=stt[0:1, 0:TO],
                    scalar1=-1.0, scalar2=None, op0=ALU.mult), waits=[t_nm], inc=True)
                rsrc, bsrc = stt16[0:1, 0:TO], stt16[0:1, TO:2 * TO]
                lhs_one = onesr16[0:1, :]
                dst = abuf
            toks = []
            for which, srcv in ((0, rsrc), (1, bsrc)):
                r, fr = rot_acquire()
                tm = A("tensor", lambda e, srcv=srcv, r=r, lhs_one=lhs_one: e.matmul(
                    psr[r][:, :], lhs_one, srcv, start=True, stop=True),
                    waits=[t_ng, fr, t_or16], inc=True)
                tv = A("vector", lambda e, which=which, r=r, dst=dst: e.tensor_copy(
                    out=dst[:, which * TO:(which + 1) * TO], in_=psr[r][:, :]),
                    waits=[tm], inc=True)
                rot_free[r] = tv
                toks.append(tv)
                tk_ln_free[0] = tm
            if dst_fn is None:
                return toks, dst
            t_apply = [None] * NI
            for o in range(NI):
                war = [apply_war_fn(o)] if apply_war_fn else []
                t_a1 = A("vector", lambda e, o=o, dst=dst: e.tensor_tensor(
                    out=dst_fn(o), in0=src_slice_fn(o),
                    in1=dst[:, 0:TO], op=ALU.mult),
                    waits=[rd_tok[o]] + toks + war, inc=True)
                t_apply[o] = A("vector", lambda e, o=o, dst=dst: e.tensor_tensor(
                    out=dst_fn(o), in0=dst_fn(o),
                    in1=dst[:, TO:2 * TO], op=ALU.add),
                    waits=[t_a1], inc=True)
            return t_apply

        def back0(c):
            pc = c % 2
            xo = pc * NI
            s = st[c]
            t_ymix = s["t_ymix"]
            t_r1 = [None] * NI
            t_op = None
            for o in range(NI):
                sl = o % 2
                tdw = A("sync", lambda e, o=o, sl=sl: e.dma_start(
                    out=wobuf[:, sl * NJ * P:(sl + 1) * NJ * P], in_=wo_ld.ap()[o]),
                    waits=[tk_wo_slot[sl]], inc=True, sem="wo0" if sl == 0 else "wo1")
                r, fr = rot_acquire()

                def op_mms(e, o=o, sl=sl, r=r):
                    last = None
                    for j in range(NJ):
                        last = e.matmul(psr[r][:, :],
                                        wobuf[:, (sl * NJ + j) * P:(sl * NJ + j + 1) * P],
                                        ymix[:, j, :], start=(j == 0), stop=(j == NJ - 1))
                    return last
                t_op = A("tensor", op_mms, waits=[tdw, fr, t_ymix[NJ - 1]], inc=True)
                tk_wo_slot[sl] = t_op
                t_r1[o] = A("vector", lambda e, o=o, r=r, xo=xo: e.tensor_tensor(
                    out=x16[:, xo + o, 3:3 + TO], in0=x16[:, xo + o, 3:3 + TO],
                    in1=psr[r][:, :], op=ALU.add), waits=[t_op], inc=True)
                rot_free[r] = t_r1[o]
            tk_ymix_free[:] = [t_op]
            s["t_r1"] = t_r1

        def back1(c):
            pc = c % 2
            xo = pc * NI
            s = st[c]
            t_r1 = s["t_r1"]
            t_x1 = layer_norm(t_r1, False,
                              dst_fn=lambda o: x1b[:, o, :],
                              src_slice_fn=lambda o, xo=xo: x16[:, xo + o, 3:3 + TO],
                              apply_war_fn=lambda o: tk_x1_free[0])
            s["t_x1"] = t_x1

        def back2(c):
            pc = c % 2
            xo = pc * NI
            s = st[c]
            t_x1 = s["t_x1"]
            t_r2 = [None] * NI
            t_ff = None
            for o in range(NI):
                sl = o % 2
                tdwf = A("sync", lambda e, o=o, sl=sl: e.dma_start(
                    out=wfbuf[:, sl * NI * P:(sl + 1) * NI * P], in_=wf_ld.ap()[o]),
                    waits=[tk_wf_slot[sl]], inc=True, sem="wf0" if sl == 0 else "wf1")
                r, fr = rot_acquire()

                def ff_mms(e, o=o, sl=sl, r=r):
                    last = None
                    for i in range(NI):
                        last = e.matmul(psr[r][:, :],
                                        wfbuf[:, (sl * NI + i) * P:(sl * NI + i + 1) * P],
                                        x1b[:, i, :], start=(i == 0), stop=(i == NI - 1))
                    return last
                t_ff = A("tensor", ff_mms, waits=[tdwf, fr, t_x1[NI - 1]], inc=True)
                tk_wf_slot[sl] = t_ff
                t_r2[o] = A("vector", lambda e, o=o, r=r, xo=xo: e.tensor_tensor(
                    out=x16[:, xo + o, 3:3 + TO], in0=x1b[:, o, :], in1=psr[r][:, :],
                    op=ALU.add), waits=[t_ff], inc=True)
                rot_free[r] = t_r2[o]
            tk_x1_free[:] = [t_ff]
            s["t_r2"] = t_r2

        def back3(c):
            pc = c % 2
            xo = pc * NI
            s = st[c]
            t_r2 = s["t_r2"]
            toks2, dst2 = layer_norm(t_r2, True, dst_fn=None,
                                     src_slice_fn=lambda o, xo=xo: x16[:, xo + o, 3:3 + TO])
            t_tp_last_per_tt = [None] * 4
            t_o2_last = None
            t0 = c * TO
            for o in range(NI):
                sl = o % 2
                t_oa = A("vector", lambda e, o=o, sl=sl, xo=xo: e.tensor_tensor(
                    out=outf[:, sl * TO:(sl + 1) * TO],
                    in0=x16[:, xo + o, 3:3 + TO], in1=dst2[:, 0:TO],
                    op=ALU.mult),
                    waits=[t_r2[o]] + toks2
                    + ([tk_outf_slot[sl]] if tk_outf_slot[sl] else []), inc=True)
                t_o2_last = A("vector", lambda e, sl=sl: e.tensor_tensor(
                    out=outf[:, sl * TO:(sl + 1) * TO],
                    in0=outf[:, sl * TO:(sl + 1) * TO],
                    in1=dst2[:, TO:2 * TO], op=ALU.add),
                    waits=[t_oa], inc=True)
                for tt in range(4):
                    r, fr = rot_acquire()
                    tm = A("tensor", lambda e, tt=tt, r=r, sl=sl: e.transpose(
                        psr[r][:, :P], outf[:, sl * TO + tt * P:sl * TO + (tt + 1) * P],
                        ident[:, :]), waits=[t_o2_last, fr], inc=True)
                    ta = A("scalar", lambda e, o=o, tt=tt, r=r: e.activation(
                        out=otok[:, tt * DM + o * P:tt * DM + (o + 1) * P],
                        in_=psr[r][:, :P], func=AF.Copy),
                        waits=[tm] + ([tk_ot_slot[tt]] if tk_ot_slot[tt] else []), inc=True)
                    rot_free[r] = ta
                    t_tp_last_per_tt[tt] = ta
                    tk_outf_slot[sl] = tm
            for tt in range(4):
                tdo = A("gpsimd", lambda e, t0=t0, tt=tt: e.dma_start(
                    out=yout[t0 + tt * P:t0 + (tt + 1) * P, :],
                    in_=otok[:, tt * DM:(tt + 1) * DM]),
                    waits=[t_tp_last_per_tt[tt]], inc=True, inck=16,
                    sem="gt%d" % tt)
                tk_ot_slot[tt] = tdo
            tk_x16_pp[pc] = [t_o2_last, tk_sq_slot[0], tk_sq_slot[1]]

        front_a(0)
        for j in range(NJ):
            front_b(0, j)
        front_tail(0)
        for c in range(NCH):
            if c > 0:
                back0(c - 1)
            hooks = {}
            if c > 0:
                hooks[1] = lambda cc=c - 1: back1(cc)
                hooks[2] = lambda cc=c - 1: back2(cc)
                hooks[3] = lambda cc=c - 1: back3(cc)
            if c + 1 < NCH:
                hooks[4] = lambda cc=c + 1: front_a(cc)
                hooks[5] = lambda cc=c + 1: [front_b(cc, _j) for _j in range(6)]
                hooks[6] = lambda cc=c + 1: [front_b(cc, _j) for _j in range(6, 12)]
            scan(c, hooks)
        back0(NCH - 1)
        back1(NCH - 1)
        back2(NCH - 1)
        back3(NCH - 1)

        finals = [(s, pr.counts[s]) for s in ("sync",) + Prog.CHANNELS
                  if pr.counts[s] > 0]

        def final_waits(e):
            for s, v in finals:
                e.wait_ge(pr.sems[s], v)
        A("sync", final_waits)

        block = ent(nc.Block())
        pr.emit(block)
    return nc


_NC_CACHE = [None]


def kernel(L0_attn_x, L0_attn_W_in, L0_attn_conv_k, L0_attn_conv_b,
           L0_attn_W_xproj, L0_attn_W_dt, L0_attn_b_dt, L0_attn_A_log,
           L0_attn_D, L0_attn_W_out, L0_ffn_weight):
    f32 = np.float32
    x = np.asarray(L0_attn_x, f32)
    W_in = np.ascontiguousarray(np.asarray(L0_attn_W_in, f32))
    ck = np.ascontiguousarray(np.asarray(L0_attn_conv_k, f32)[:, 0, :])
    cb = np.ascontiguousarray(np.asarray(L0_attn_conv_b, f32).reshape(NJ, P).T)
    wxp = np.ascontiguousarray(
        np.asarray(L0_attn_W_xproj, f32).reshape(NJ, P, 160).transpose(1, 0, 2).reshape(P, NJ * 160))
    wdt = np.ascontiguousarray(np.asarray(L0_attn_W_dt, f32))
    bdt = np.ascontiguousarray(np.asarray(L0_attn_b_dt, f32).reshape(NJ, P).T)
    ap_ = np.ascontiguousarray(
        np.asarray(L0_attn_A_log, f32).reshape(NJ, P, NST).transpose(1, 0, 2).reshape(P, NJ * NST))
    dp = np.ascontiguousarray(np.asarray(L0_attn_D, f32).reshape(NJ, P).T)
    wout = np.ascontiguousarray(np.asarray(L0_attn_W_out, f32))
    wffn = np.ascontiguousarray(np.asarray(L0_ffn_weight, f32))
    ident = np.eye(P, dtype=f32)
    onescol = np.zeros((P, 2), f32)
    onescol[:, 0] = 1.0
    onesrow = np.ones((1, P), f32)
    selc = np.zeros((32, 2 * DI), f32)
    for n in range(NST):
        selc[n, n * P:(n + 1) * P] = -1.0
        selc[NST + n, DI + n * P:DI + (n + 1) * P] = 1.0

    if _NC_CACHE[0] is None:
        _NC_CACHE[0] = build_nc()
    nc = _NC_CACHE[0]

    shared = dict(w_in=W_in, conv_k=ck, cb_perm=cb, wxp_perm=wxp, w_dt=wdt,
                  bdt_perm=bdt, a_perm=ap_, d_perm=dp, w_out=wout, w_ffn=wffn,
                  identc=ident, onescol=onescol, onesrow=onesrow, selc=selc)
    in_maps = [dict(shared, xin=np.ascontiguousarray(x[b])) for b in range(8)]
    res = run_bass_kernel_spmd(nc, in_maps, core_ids=list(range(8)))
    out = np.stack([np.asarray(res.results[b]["yout"]) for b in range(8)], axis=0)
    return out
